# version 02
# baseline (speedup 1.0000x reference)
import numpy as np
import ml_dtypes as mld
import concourse.bass as bass
import concourse.mybir as mybir
from concourse.bass_utils import run_bass_kernel_spmd
from concourse.alu_op_type import AluOpType as Op

F32 = mybir.dt.float32
BF16 = mybir.dt.bfloat16
I32 = mybir.dt.int32
AF = mybir.ActivationFunctionType

NCORE = 8
M = 1024
D = 4096
FI = 11008
NKT = D // 128
NNT = FI // 128
NMT = M // 128
NDC = D // 512
OCTS = [list(range(o * 8, min(o * 8 + 8, NNT))) for o in range((NNT + 7) // 8)]


class Sched:
    ENGINES = ("sync", "gpsimd", "scalar", "vector", "tensor")

    def __init__(self):
        self.ev = {e: [] for e in self.ENGINES}
        self.cnt = {}

    def emit(self, eng, fn, waits=(), incs=()):
        for s, v in waits:
            assert v <= self.cnt.get(s, 0), \
                f"[{eng}] wait {s}>={v} but serial count {self.cnt.get(s, 0)}"
        self.ev[eng].append((fn, [(s, v) for s, v in waits if v > 0], list(incs)))
        for s, amt in incs:
            self.cnt[s] = self.cnt.get(s, 0) + amt

    def dma(self, queue, out, in_, semname, waits=(), slow=False):
        self.emit(queue,
                  lambda e, o=out, i=in_, sl=slow: e.dma_start(
                      out=o, in_=i, allow_slow_non_contiguous=sl),
                  waits=waits, incs=[(semname, 16)])
        return self.cnt[semname]

    def c(self, s):
        return self.cnt.get(s, 0)


def ap_with(ap, dims, extra_offset=0):
    return bass.AP(tensor=ap.tensor, offset=ap.offset + extra_offset,
                   ap=[[int(d[0]), int(d[1])] for d in dims])


def part_dim(ap):
    return [int(ap.ap[0][0]), int(ap.ap[0][1])]


DEBUG = False


def build_nc():
    nc = bass.Bass("TRN2", target_bir_lowering=False)
    dbg = {"kind": "ExternalOutput"} if DEBUG else {}

    x_in = nc.dram_tensor("x0", [NMT, 128, D], F32, kind="ExternalInput").ap()
    w_t = nc.dram_tensor("w_t", [NKT, NKT, 128, 128], F32, kind="ExternalInput").ap()
    m_t = nc.dram_tensor("m_t", [NKT, NKT, 128, 128], I32, kind="ExternalInput").ap()
    qg_i = nc.dram_tensor("qg", [NNT, 4, 128, 8, 16], I32, kind="ExternalInput").ap()
    qu_i = nc.dram_tensor("qu", [NNT, 4, 128, 8, 16], I32, kind="ExternalInput").ap()
    zg_i = nc.dram_tensor("zg", [NNT, 4, 128, 8, 16], I32, kind="ExternalInput").ap()
    zu_i = nc.dram_tensor("zu", [NNT, 4, 128, 8, 16], I32, kind="ExternalInput").ap()
    sg_i = nc.dram_tensor("sg", [NNT, 4, 128, 8, 128], F32, kind="ExternalInput").ap()
    su_i = nc.dram_tensor("su", [NNT, 4, 128, 8, 128], F32, kind="ExternalInput").ap()
    qd_i = nc.dram_tensor("qd", [NDC, NNT, 128, 64], I32, kind="ExternalInput").ap()
    zd_i = nc.dram_tensor("zd", [NDC, NNT, 128, 64], I32, kind="ExternalInput").ap()
    sd_i = nc.dram_tensor("sd", [NDC, NNT, 128, 512], F32, kind="ExternalInput").ap()
    cvec = nc.dram_tensor("cvec", [1, D], F32, kind="ExternalInput").ap()
    wmod = nc.dram_tensor("wmod", [D, 3072], F32, kind="ExternalInput").ap()
    ones_in = nc.dram_tensor("ones_in", [1, 128], F32, kind="ExternalInput").ap()
    ident_in = nc.dram_tensor("ident_in", [128, 128], F32, kind="ExternalInput").ap()
    y_out = nc.dram_tensor("y", [NMT, 128, D], F32, kind="ExternalOutput").ap()

    mod_in_t = nc.dram_tensor("mod_in", [3072], F32)
    gbrow_t = nc.dram_tensor("gbrow", [4096], F32)
    mod_out_t = nc.dram_tensor("mod_out", [6 * D], F32)
    mod_dbg_t = nc.dram_tensor("mod_dbg", [6 * D], F32, **dbg) if DEBUG else None
    csi_dbg_t = nc.dram_tensor("csi_dbg", [128, NKT], F32, **dbg) if DEBUG else None
    modin_dbg_t = nc.dram_tensor("modin_dbg", [1, 3072], F32, **dbg) if DEBUG else None
    vecs_dbg_t = nc.dram_tensor("vecs_dbg", [128, 6 * NKT], F32, **dbg) if DEBUG else None
    h1h_dbg_t = nc.dram_tensor("h1h_dbg", [128, NKT * M], BF16, **dbg) if DEBUG else None
    h1l_dbg_t = nc.dram_tensor("h1l_dbg", [128, NKT * M], BF16, **dbg) if DEBUG else None
    wmf_dbg_t = nc.dram_tensor("wmf_dbg", [128, 128], F32, **dbg) if DEBUG else None
    h2h_dbg_t = nc.dram_tensor("h2h_dbg", [128, NKT * M], BF16, **dbg) if DEBUG else None
    h2l_dbg_t = nc.dram_tensor("h2l_dbg", [128, NKT * M], BF16, **dbg) if DEBUG else None
    ln2_dbg_t = nc.dram_tensor("ln2_dbg", [128, 2048], F32, **dbg) if DEBUG else None
    wmth_dbg_t = nc.dram_tensor("wmth_dbg", [128, 128], BF16, **dbg) if DEBUG else None
    wmtl_dbg_t = nc.dram_tensor("wmtl_dbg", [128, 128], BF16, **dbg) if DEBUG else None
    x1T_t = nc.dram_tensor("x1T_s", [NKT, 128, M], F32, **dbg)
    x2_t = nc.dram_tensor("x2_s", [NKT, 128, M], F32, **dbg)
    hidh_t = nc.dram_tensor("hidh_s", [NNT, 128, M], BF16, **dbg)
    hidl_t = nc.dram_tensor("hidl_s", [NNT, 128, M], BF16, **dbg)
    mod_in, mod_out = mod_in_t.ap(), mod_out_t.ap()
    gbrow = gbrow_t.ap()
    x1T, x2s, hidh, hidl = x1T_t.ap(), x2_t.ap(), hidh_t.ap(), hidl_t.ap()

    S = Sched()
    WQ, AQ, SQ, MMS, VES, SES, CCS = "d_wq", "d_aq", "d_sq", "s_mm", "s_ve", "s_se", "s_cc"

    import contextlib
    stack = contextlib.ExitStack()
    with stack:
        ent = stack.enter_context
        h_h = ent(nc.sbuf_tensor("h_h", [128, NKT * M], BF16))
        h_l = ent(nc.sbuf_tensor("h_l", [128, NKT * M], BF16))
        scr = ent(nc.sbuf_tensor("scr", [128, 4096], F32))
        bct = ent(nc.sbuf_tensor("bct", [128, 4096], F32))
        ring = ent(nc.sbuf_tensor("ring", [128, 17408], BF16))
        iscr = ent(nc.sbuf_tensor("iscr", [128, 2304], I32))
        mod_km = ent(nc.sbuf_tensor("mod_km", [128, 6 * NKT], F32))
        c_km = ent(nc.sbuf_tensor("c_km", [128, NKT], F32))
        c_si = ent(nc.sbuf_tensor("c_si", [128, NKT], F32))
        vecs = ent(nc.sbuf_tensor("vecs", [128, 6 * NKT], F32))
        ones_t = ent(nc.sbuf_tensor("ones_t", [1, 128], F32))
        ones_c = ent(nc.sbuf_tensor("ones_c", [128, 1], F32))
        ident = ent(nc.sbuf_tensor("ident", [128, 128], F32))
        eps_t = ent(nc.sbuf_tensor("eps_t", [128, 1], F32))
        ps0 = ent(nc.psum_tensor("ps0", [128, 512], F32))
        ps1 = ent(nc.psum_tensor("ps1", [128, 512], F32))
        ps2 = ent(nc.psum_tensor("ps2", [128, 512], F32))
        ps3 = ent(nc.psum_tensor("ps3", [128, 512], F32))
        ps4 = ent(nc.psum_tensor("ps4", [128, 512], F32))
        ps5 = ent(nc.psum_tensor("ps5", [128, 512], F32))
        ps6 = ent(nc.psum_tensor("ps6", [128, 512], F32))
        ps7 = ent(nc.psum_tensor("ps7", [128, 512], F32))
        d_wq = ent(nc.semaphore("d_wq"))
        d_aq = ent(nc.semaphore("d_aq"))
        d_sq = ent(nc.semaphore("d_sq"))
        s_mm = ent(nc.semaphore("s_mm"))
        s_ve = ent(nc.semaphore("s_ve"))
        s_se = ent(nc.semaphore("s_se"))
        s_cc = ent(nc.semaphore("s_cc"))
        block = ent(nc.Block())
        PS = [ps0, ps1, ps2, ps3, ps4, ps5, ps6, ps7]

        S.dma("sync", ones_t[:, :], ones_in[:, :], WQ)
        S.dma("sync", ident[:, :], ident_in[:, :], WQ)
        S.emit("vector", lambda e: e.memset(ones_c[:, :], 1.0), incs=[(VES, 1)])
        S.emit("vector", lambda e: e.memset(eps_t[:, :], 1e-5), incs=[(VES, 1)])
        const_wq = S.c(WQ)

        S.dma("gpsimd", c_km[:, :], ap_with(cvec, [[1, 128], [128, NKT]]), AQ, slow=True)
        S.emit("scalar", lambda e: e.activation(out=c_si[:, :], in_=c_km[:, :], func=AF.Silu),
               waits=[(AQ, S.c(AQ))], incs=[(SES, 1)])
        silu_se = S.c(SES)

        wmbuf = [scr[:, 0:512], scr[:, 512:1024]]
        mm_base = S.c(MMS)
        for ch in range(6):
            for kt in range(NKT):
                idx = ch * NKT + kt
                waits = [(WQ, const_wq)] if idx == 0 else []
                if idx >= 2:
                    waits.append((MMS, mm_base + idx - 1))
                wl = S.dma("sync", wmbuf[idx % 2][:, :],
                           ap_with(wmod, [[3072, 128], [1, 512]], kt * 128 * 3072 + ch * 512),
                           WQ, waits=waits)
                mmw = [(WQ, wl)]
                if idx == 0:
                    mmw.append((SES, silu_se))
                S.emit("tensor",
                       lambda e, c_=ch, k_=kt, s_=idx % 2: e.matmul(
                           PS[c_][0:1, :], c_si[:, k_:k_ + 1], wmbuf[s_][:, :],
                           start=(k_ == 0), stop=(k_ == NKT - 1)),
                       waits=mmw, incs=[(MMS, 1)])

        modsl = bct[0:1, 0:3072]
        for ch in range(6):
            S.emit("vector",
                   lambda e, c_=ch: e.tensor_copy(modsl[:, c_ * 512:(c_ + 1) * 512],
                                                  PS[c_][0:1, :]),
                   waits=[(MMS, mm_base + ch * NKT + NKT)], incs=[(VES, 1)])
        p0_ve = S.c(VES)
        if DEBUG:
            S.dma("gpsimd", csi_dbg_t.ap(), c_si[:, :], AQ, waits=[(SES, silu_se)])
            S.dma("gpsimd", modin_dbg_t.ap(), modsl[:, :], AQ, waits=[(VES, p0_ve)])
        S.dma("gpsimd", ap_with(mod_in, [[3072, 1], [1, 3072]]), modsl[:, :],
              AQ, waits=[(VES, p0_ve)])
        S.emit("gpsimd",
               lambda e: e.collective_compute(
                   "AllGather", Op.bypass, replica_groups=[list(range(NCORE))],
                   ins=[mod_in_t.ap().opt()], outs=[mod_out_t.ap().opt()]),
               waits=[(AQ, S.c(AQ))], incs=[(CCS, 1)])
        mtmp = [scr[:, 0:128], scr[:, 128:256]]
        mod_tr_se = 0
        for hvi in range(2):
            S.dma("gpsimd", mtmp[hvi][0:96, :],
                  ap_with(mod_out, [[128, 96], [1, 128]], hvi * 12288), AQ,
                  waits=[(CCS, 1), (MMS, mm_base + 6 * NKT)])
            mld = S.c(AQ)
            S.emit("tensor", lambda e, h_=hvi: e.matmul(
                PS[6][:, 0:96], mtmp[h_][0:96, :], ident[0:96, 0:96],
                is_transpose=True, start=True, stop=True),
                waits=[(AQ, mld), (WQ, const_wq)] +
                      ([(SES, mod_tr_se)] if hvi else []), incs=[(MMS, 1)])
            S.emit("scalar", lambda e, h_=hvi: e.copy(
                out=mod_km[:, h_ * 96:(h_ + 1) * 96], in_=PS[6][:, 0:96]),
                waits=[(MMS, S.c(MMS))], incs=[(SES, 1)])
            mod_tr_se = S.c(SES)
        mod_aq = S.c(AQ)
        mod_se = S.c(SES)

        K = NKT
        def drv(fn):
            S.emit("vector", fn, waits=[(AQ, mod_aq), (SES, mod_se)], incs=[(VES, 1)])
        drv(lambda e: e.tensor_copy(vecs[:, 0:K], mod_km[:, 0:K]))
        drv(lambda e: e.tensor_scalar_add(vecs[:, K:2 * K], mod_km[:, K:2 * K], 1.0))
        drv(lambda e: e.tensor_copy(vecs[:, 3 * K:4 * K], mod_km[:, 3 * K:4 * K]))
        drv(lambda e: e.tensor_scalar_add(vecs[:, 4 * K:5 * K], mod_km[:, 4 * K:5 * K], 1.0))
        for off in (2 * K, 5 * K):
            g_ap = mod_km[:, off:off + K]
            o_ap = vecs[:, off:off + K]
            t1 = scr[:, 1024:1024 + K]
            t2 = scr[:, 1024 + K:1024 + 2 * K]
            mk = scr[:, 1024 + 2 * K:1024 + 3 * K]
            drv(lambda e, g=g_ap, t=t1: e.tensor_scalar_max(t, g, 0.25))
            drv(lambda e, g=g_ap, t=t2: e.tensor_scalar_min(t, g, -0.25))
            drv(lambda e, g=g_ap, m_=mk: e.tensor_scalar(m_, g, 0.0, None, Op.is_ge))
            drv(lambda e, a=t1, b=t2, o=o_ap: e.tensor_tensor(o, a, b, Op.subtract))
            drv(lambda e, m_=mk, o=o_ap: e.tensor_tensor(o, o, m_, Op.mult))
            drv(lambda e, b=t2, o=o_ap: e.tensor_tensor(o, o, b, Op.add))
        vec_ve = S.c(VES)
        if DEBUG:
            S.dma("gpsimd", vecs_dbg_t.ap(), vecs[:, :], AQ, waits=[(VES, vec_ve)])
        sa_v = vecs[:, 0:K]; ca1_v = vecs[:, K:2 * K]; ga_v = vecs[:, 2 * K:3 * K]
        sb_v = vecs[:, 3 * K:4 * K]; cb1_v = vecs[:, 4 * K:5 * K]; gb_v = vecs[:, 5 * K:6 * K]

        def bcast_row(row_ap, dst, width, extra_wait=()):
            nch = (width + 511) // 512
            for chi in range(nch):
                w_ = min(512, width - chi * 512)
                b_ = PS[4 + (chi % 2)]
                waits = list(extra_wait) if chi == 0 else []
                if chi >= 2:
                    waits.append((SES, S.c(SES) - 1))
                S.emit("tensor",
                       lambda e, r=row_ap, c_=chi, ww=w_, b2=b_: e.matmul(
                           b2[:, 0:ww], ones_t[:, :], r[:, c_ * 512:c_ * 512 + ww],
                           start=True, stop=True),
                       waits=waits, incs=[(MMS, 1)])
                S.emit("scalar",
                       lambda e, c_=chi, ww=w_, b2=b_, d=dst: e.copy(
                           out=d[:, c_ * 512:c_ * 512 + ww], in_=b2[:, 0:ww]),
                       waits=[(MMS, S.c(MMS))], incs=[(SES, 1)])

        xa = scr[:, 0:2048]
        xch = [scr[:, 2048:2176], scr[:, 2176:2304]]
        sqc = [scr[:, 2304:2432], scr[:, 2432:2560]]
        evac_se = {}
        statx_mm = {}
        statq_mm = {}
        x1t_aq = {}
        for mt in range(NMT):
            xl = 0
            for kt in range(NKT):
                gi = mt * NKT + kt
                if kt % 16 == 0:
                    if gi == 0:
                        waits = [(VES, vec_ve)]
                    else:
                        waits = [(SES, evac_se[gi - 1])]
                    S.dma("gpsimd", xa[:, :],
                          ap_with(x_in, [[D, 128], [1, 2048]],
                                  mt * 128 * D + (kt // 16) * 2048),
                          AQ, waits=waits)
                    xl = S.c(AQ)
                b_ = PS[6 + (gi % 2)]
                slot = gi % 2
                waits = [(AQ, xl)]
                if gi >= 2:
                    waits.append((SES, evac_se[gi - 2]))
                S.emit("tensor",
                       lambda e, k_=kt, b2=b_: e.matmul(
                           b2[:, 0:128], xa[:, (k_ % 16) * 128:(k_ % 16) * 128 + 128],
                           ident[:, :], is_transpose=True, start=True, stop=True),
                       waits=waits, incs=[(MMS, 1)])
                trmm = S.c(MMS)
                waits = [(MMS, trmm)]
                if gi >= 2:
                    waits += [(MMS, statx_mm[gi - 2]), (AQ, x1t_aq[gi - 2])]
                S.emit("scalar",
                       lambda e, s_=slot, b2=b_: e.copy(out=xch[s_][:, :], in_=b2[:, 0:128]),
                       waits=waits, incs=[(SES, 1)])
                evac_se[gi] = S.c(SES)
                waits = []
                if gi >= 2:
                    waits.append((MMS, statq_mm[gi - 2]))
                S.emit("scalar",
                       lambda e, s_=slot: e.activation(out=sqc[s_][:, :], in_=xch[s_][:, :],
                                                       func=AF.Square),
                       waits=waits, incs=[(SES, 1)])
                sqe = S.c(SES)
                half, mcol = mt // 4, (mt % 4) * 128
                fi_ = (kt == 0 and mt % 4 == 0)
                la_ = (kt == NKT - 1 and mt % 4 == 3)
                waits = [(SES, evac_se[gi])]
                if gi == 0:
                    waits.append((VES, p0_ve))
                S.emit("tensor",
                       lambda e, s_=slot, h_=half, mc=mcol, f=fi_, l=la_: e.matmul(
                           PS[h_][0:1, mc:mc + 128], ones_c[:, :], xch[s_][:, :],
                           start=f, stop=l),
                       waits=waits, incs=[(MMS, 1)])
                statx_mm[gi] = S.c(MMS)
                S.emit("tensor",
                       lambda e, s_=slot, h_=half, mc=mcol, f=fi_, l=la_: e.matmul(
                           PS[2 + h_][0:1, mc:mc + 128], ones_c[:, :], sqc[s_][:, :],
                           start=f, stop=l),
                       waits=[(SES, sqe)], incs=[(MMS, 1)])
                statq_mm[gi] = S.c(MMS)
                S.dma("gpsimd", x1T[kt, :, mt * 128:(mt + 1) * 128], xch[slot][:, :],
                      AQ, waits=[(SES, evac_se[gi])])
                x1t_aq[gi] = S.c(AQ)
        stats1_mm = S.c(MMS)
        x1T_aq_all = S.c(AQ)

        def ln_rows(wait_mm):
            mean = bct[0:1, 0:1024]
            var = bct[0:1, 1024:2048]
            rstd = bct[0:1, 2048:3072]
            nmrs = bct[0:1, 3072:4096]
            S.emit("vector", lambda e: e.tensor_copy(mean[:, 0:512], PS[0][0:1, :]),
                   waits=[(MMS, wait_mm)], incs=[(VES, 1)])
            S.emit("vector", lambda e: e.tensor_copy(mean[:, 512:1024], PS[1][0:1, :]), incs=[(VES, 1)])
            S.emit("vector", lambda e: e.tensor_copy(var[:, 0:512], PS[2][0:1, :]), incs=[(VES, 1)])
            S.emit("vector", lambda e: e.tensor_copy(var[:, 512:1024], PS[3][0:1, :]), incs=[(VES, 1)])
            S.emit("vector", lambda e: e.tensor_scalar_mul(mean[:, :], mean[:, :], 1.0 / D), incs=[(VES, 1)])
            S.emit("vector", lambda e: e.tensor_scalar_mul(var[:, :], var[:, :], 1.0 / D), incs=[(VES, 1)])
            S.emit("vector", lambda e: e.tensor_tensor(rstd[:, :], mean[:, :], mean[:, :], Op.mult),
                   incs=[(VES, 1)])
            S.emit("vector", lambda e: e.tensor_tensor(var[:, :], var[:, :], rstd[:, :], Op.subtract),
                   incs=[(VES, 1)])
            vv = S.c(VES)
            S.emit("scalar", lambda e: e.activation(out=rstd[:, :], in_=var[:, :], func=AF.Sqrt,
                                                    bias=eps_t[0:1, :]),
                   waits=[(VES, vv)], incs=[(SES, 1)])
            S.emit("vector", lambda e: e.reciprocal(rstd[:, :], rstd[:, :]),
                   waits=[(SES, S.c(SES))], incs=[(VES, 1)])
            S.emit("vector", lambda e: e.scalar_tensor_tensor(nmrs[:, :], mean[:, :], -1.0, rstd[:, :],
                                                              Op.mult, Op.mult), incs=[(VES, 1)])
            return rstd, nmrs, S.c(VES)

        rstd_b = bct[:, 0:1024]
        nmrs_b = bct[:, 1024:2048]

        def apply_pass(xT_src, ca1_col, sa_col, first_wait):
            bufs = [scr[:, 0:1024], scr[:, 1024:2048]]
            ht = scr[:, 2048:3072]
            buf_free = {}
            for kt in range(NKT):
                slot = kt % 2
                waits = list(first_wait) if kt == 0 else []
                if kt >= 2:
                    waits.append((VES, buf_free[kt - 2]))
                S.dma("gpsimd", bufs[slot][:, :], xT_src[kt], AQ, waits=waits)
                ld = S.c(AQ)
                S.emit("vector", lambda e, s_=slot: e.tensor_tensor(
                    bufs[s_][:, :], bufs[s_][:, :], rstd_b[:, :], Op.mult),
                    waits=[(AQ, ld)], incs=[(VES, 1)])
                S.emit("vector", lambda e, s_=slot: e.tensor_tensor(
                    ht[:, :], bufs[s_][:, :], nmrs_b[:, :], Op.add),
                    waits=[(VES, S.c(VES))], incs=[(VES, 1)])
                buf_free[kt] = S.c(VES)
                S.emit("scalar", lambda e, k_=kt, cc=ca1_col, ss=sa_col: e.activation(
                    out=ht[:, :], in_=ht[:, :], func=AF.Identity,
                    scale=cc[:, k_:k_ + 1], bias=ss[:, k_:k_ + 1]),
                    waits=[(VES, buf_free[kt])], incs=[(SES, 1)])
                S.emit("scalar", lambda e, k_=kt: e.copy(
                    out=h_h[:, k_ * M:(k_ + 1) * M], in_=ht[:, :]), incs=[(SES, 1)])
                hse = S.c(SES)
                S.emit("vector", lambda e, k_=kt: e.tensor_tensor(
                    h_l[:, k_ * M:(k_ + 1) * M], ht[:, :], h_h[:, k_ * M:(k_ + 1) * M],
                    Op.subtract),
                    waits=[(SES, hse)], incs=[(VES, 1)])
            return S.c(VES), S.c(SES)

        r1, n1, lnv1 = ln_rows(stats1_mm)
        bcast_row(r1, rstd_b, 1024, extra_wait=[(VES, lnv1)])
        bcast_row(n1, nmrs_b, 1024)
        h1_ve, h1_se = apply_pass(x1T, ca1_v, sa_v,
                                  [(SES, S.c(SES)), (AQ, x1T_aq_all)])
        if DEBUG:
            S.dma("gpsimd", h1h_dbg_t.ap(), h_h[:, :], AQ, waits=[(VES, h1_ve), (SES, h1_se)])
            S.dma("gpsimd", h1l_dbg_t.ap(), h_l[:, :], AQ)

        wnat = [scr[:, 3072:3200], scr[:, 3200:3328],
                scr[:, 3328:3456], scr[:, 3456:3584]]
        mnat = [iscr[:, 0:128], iscr[:, 128:256],
                iscr[:, 256:384], iscr[:, 384:512]]
        wm_f = [scr[:, 3584:3712], scr[:, 3712:3840]]
        wmt_h = [ring[:, i * 128:(i + 1) * 128] for i in range(3)]
        wmt_l = [ring[:, (3 + i) * 128:(4 + i) * 128] for i in range(3)]
        x1c = [scr[:, 0:512], scr[:, 512:1024]]
        x2c = [scr[:, 1024:1536], scr[:, 1536:2048]]

        tt_ve, tr_mm, hi_se, lo_ve, accmm = {}, {}, {}, {}, {}
        x2st_aq = {}
        dbg_hold = {}
        epi_ve = {}
        for nt in range(NKT):
            accA = PS[0 + 2 * (nt % 2)]
            accB = PS[1 + 2 * (nt % 2)]
            for mc in range(2):
                waits = []
                if nt == 0 and mc == 0:
                    waits.append((VES, h1_ve))
                if nt >= 1:
                    waits.append((VES, epi_ve[nt - 1]))
                S.dma("gpsimd", x1c[mc][:, :], x1T[nt, :, mc * 512:(mc + 1) * 512], AQ,
                      waits=waits)
            x1_ld = S.c(AQ)
            for kt in range(NKT):
                gi = nt * NKT + kt
                s3, s2 = gi % 3, gi % 2
                s4 = gi % 4
                waits = []
                if gi == 0:
                    waits.append((MMS, stats1_mm))
                if gi >= 4:
                    waits.append((VES, tt_ve[gi - 4]))
                S.dma("sync", wnat[s4][:, :], w_t[nt, kt], WQ, waits=waits)
                S.dma("sync", mnat[s4][:, :], m_t[nt, kt], WQ)
                wl = S.c(WQ)
                waits = [(WQ, wl)]
                if gi >= 2:
                    waits.append((MMS, tr_mm[gi - 2]))
                if DEBUG and gi == 2 and 0 in dbg_hold:
                    waits.append((AQ, dbg_hold[0]))
                S.emit("vector", lambda e, s_=s2, s4_=s4: e.tensor_tensor(
                    wm_f[s_][:, :], wnat[s4_][:, :], mnat[s4_][:, :], Op.mult),
                    waits=waits, incs=[(VES, 1)])
                tt_ve[gi] = S.c(VES)
                trb = PS[6 + (gi % 2)]
                waits = [(VES, tt_ve[gi])]
                if gi == 0:
                    waits.append((SES, h1_se))
                if gi >= 2:
                    waits += [(SES, hi_se[gi - 2]), (VES, lo_ve[gi - 2])]
                S.emit("tensor", lambda e, s_=s2, b2=trb: e.matmul(
                    b2[:, 0:128], wm_f[s_][:, :], ident[:, :], is_transpose=True,
                    start=True, stop=True),
                    waits=waits, incs=[(MMS, 1)])
                tr_mm[gi] = S.c(MMS)
                waits = [(MMS, tr_mm[gi])]
                if gi >= 3:
                    waits.append((MMS, accmm[gi - 3]))
                if DEBUG and gi == 3 and 0 in dbg_hold:
                    waits.append((AQ, dbg_hold[0]))
                S.emit("scalar", lambda e, s_=s3, b2=trb: e.copy(
                    out=wmt_h[s_][:, :], in_=b2[:, 0:128]),
                    waits=waits, incs=[(SES, 1)])
                hi_se[gi] = S.c(SES)
                S.emit("vector", lambda e, s_=s3, b2=trb: e.tensor_tensor(
                    wmt_l[s_][:, :], b2[:, 0:128], wmt_h[s_][:, :], Op.subtract),
                    waits=[(SES, hi_se[gi])], incs=[(VES, 1)])
                lo_ve[gi] = S.c(VES)
                if DEBUG and gi == 0:
                    S.dma("gpsimd", wmf_dbg_t.ap(), wm_f[0][:, :], AQ,
                          waits=[(VES, tt_ve[0])])
                    S.dma("gpsimd", wmth_dbg_t.ap(), wmt_h[0][:, :], AQ,
                          waits=[(SES, hi_se[0])])
                    S.dma("gpsimd", wmtl_dbg_t.ap(), wmt_l[0][:, :], AQ,
                          waits=[(VES, lo_ve[0])])
                    dbg_hold[0] = S.c(AQ)
                hh = lambda mc_, k_=kt: h_h[:, k_ * M + mc_ * 512: k_ * M + mc_ * 512 + 512]
                hl = lambda mc_, k_=kt: h_l[:, k_ * M + mc_ * 512: k_ * M + mc_ * 512 + 512]
                first, last = (kt == 0), (kt == NKT - 1)
                seq = [(wmt_h[s3], hh(0), accA, first, False),
                       (wmt_h[s3], hh(1), accB, first, False),
                       (wmt_h[s3], hl(0), accA, False, False),
                       (wmt_h[s3], hl(1), accB, False, False),
                       (wmt_l[s3], hh(0), accA, False, last),
                       (wmt_l[s3], hh(1), accB, False, last)]
                for i_, (wv, hv, acc, st, sp) in enumerate(seq):
                    waits = []
                    if i_ == 0:
                        waits = [(SES, hi_se[gi])]
                        if kt == 0 and nt >= 2:
                            waits.append((VES, epi_ve[nt - 2]))
                    if i_ == 4:
                        waits = [(VES, lo_ve[gi])]
                    S.emit("tensor", lambda e, w_=wv, h2=hv, a_=acc, st_=st, sp_=sp: e.matmul(
                        a_[:, :], w_, h2, start=st_, stop=sp_),
                        waits=waits, incs=[(MMS, 1)])
                accmm[gi] = S.c(MMS)
            for mc, acc in ((0, accA), (1, accB)):
                waits = [(MMS, accmm[nt * NKT + NKT - 1]), (AQ, x1_ld)]
                if nt >= 1:
                    waits.append((AQ, x2st_aq[nt - 1]))
                S.emit("vector", lambda e, n_=nt, a_=acc, mc_=mc: e.scalar_tensor_tensor(
                    x2c[mc_][:, :], a_[:, :], ga_v[:, n_:n_ + 1], x1c[mc_][:, :],
                    Op.mult, Op.add),
                    waits=waits,
                    incs=[(VES, 1)])
                S.dma("gpsimd", x2s[nt, :, mc * 512:(mc + 1) * 512], x2c[mc][:, :], AQ,
                      waits=[(VES, S.c(VES))])
            epi_ve[nt] = S.c(VES)
            x2st_aq[nt] = S.c(AQ)
        x2_aq = S.c(AQ)
        attn_mm = S.c(MMS)

        bufs3 = [scr[:, 0:1024], scr[:, 1024:2048]]
        sqt = scr[:, 2048:3072]
        last_stat = {}
        for kt in range(NKT):
            slot = kt % 2
            waits = [(AQ, x2_aq), (MMS, attn_mm)] if kt == 0 else []
            if kt >= 2:
                waits.append((MMS, last_stat[kt - 2]))
            S.dma("gpsimd", bufs3[slot][:, :], x2s[kt], AQ, waits=waits)
            ld = S.c(AQ)
            sq_w = [(AQ, ld)]
            if kt >= 1:
                sq_w.append((MMS, last_stat[kt - 1]))
            S.emit("scalar", lambda e, s_=slot: e.activation(out=sqt[:, :], in_=bufs3[s_][:, :],
                                                             func=AF.Square),
                   waits=sq_w, incs=[(SES, 1)])
            sqse = S.c(SES)
            for mc in range(2):
                waits = [(AQ, ld)]
                if kt == 0 and mc == 0:
                    waits.append((VES, lnv1))
                S.emit("tensor", lambda e, s_=slot, mc_=mc, k_=kt: e.matmul(
                    PS[mc_][0:1, :], ones_c[:, :], bufs3[s_][:, mc_ * 512:(mc_ + 1) * 512],
                    start=(k_ == 0), stop=(k_ == NKT - 1)),
                    waits=waits, incs=[(MMS, 1)])
                S.emit("tensor", lambda e, mc_=mc, k_=kt: e.matmul(
                    PS[2 + mc_][0:1, :], ones_c[:, :], sqt[:, mc_ * 512:(mc_ + 1) * 512],
                    start=(k_ == 0), stop=(k_ == NKT - 1)),
                    waits=[(SES, sqse)], incs=[(MMS, 1)])
            last_stat[kt] = S.c(MMS)
        stats2_mm = S.c(MMS)
        r2, n2, lnv2 = ln_rows(stats2_mm)
        bcast_row(r2, rstd_b, 1024, extra_wait=[(VES, lnv2)])
        bcast_row(n2, nmrs_b, 1024)
        if DEBUG:
            S.dma("gpsimd", ln2_dbg_t.ap(), bct[:, 0:2048], AQ,
                  waits=[(SES, S.c(SES)), (VES, S.c(VES))])
        h2_ve, h2_se = apply_pass(x2s, cb1_v, sb_v, [(SES, S.c(SES))])
        if DEBUG:
            S.dma("gpsimd", h2h_dbg_t.ap(), h_h[:, :], AQ, waits=[(VES, h2_ve), (SES, h2_se)])
            S.dma("gpsimd", h2l_dbg_t.ap(), h_l[:, :], AQ)

        gb_row2 = scr[0:1, 0:4096]
        S.dma("gpsimd", ap_with(gbrow, [[1, 128], [128, NKT]]), gb_v[:, :], AQ,
              waits=[(VES, h2_ve), (SES, h2_se)], slow=True)
        S.dma("gpsimd", gb_row2[:, :], ap_with(gbrow, [[4096, 1], [1, 4096]]), AQ,
              waits=[(AQ, S.c(AQ))])
        gb_aq = S.c(AQ)
        gb_b = bct[:, 0:4096]
        bcast_row(gb_row2, gb_b, 4096, extra_wait=[(AQ, gb_aq)])
        gb_se = S.c(SES)
        gb_mm = S.c(MMS)

        def slotset(si):
            return (ring[:, si * 8192: si * 8192 + 4096],
                    ring[:, si * 8192 + 4096: si * 8192 + 8192])
        hidb_h = ring[:, 16384:16896]
        hidb_l = ring[:, 16896:17408]
        qb = iscr[:, 0:128]
        zb = iscr[:, 128:256]
        uq = iscr[:, 256:1280]
        uz = iscr[:, 1280:2304]
        srb = scr[:, 0:1024]
        dqf = scr[:, 2048:3072]
        silt = scr[:, 3072:3584]
        hidf = scr[:, 3584:4096]

        first_p4 = True

        def deq_gu(q_i, z_i, s_i, nt, sset, prev_mm_done):
            nonlocal first_p4
            dsth, dstl = slotset(sset)
            for oc in range(4):
                waits = [(MMS, prev_mm_done), (VES, S.c(VES))]
                if first_p4:
                    waits.append((SES, gb_se))
                S.dma("sync", qb[:, :],
                      ap_with(q_i, [[128, 128], [1, 128]], nt * 65536 + oc * 16384),
                      WQ, waits=waits)
                S.dma("sync", zb[:, :],
                      ap_with(z_i, [[128, 128], [1, 128]], nt * 65536 + oc * 16384), WQ)
                qzl = S.c(WQ)
                sw = [(VES, S.c(VES))]
                if first_p4:
                    sw += [(SES, gb_se), (MMS, gb_mm)]
                    first_p4 = False
                S.dma("scalar", srb[:, :],
                      ap_with(s_i, [[1024, 128], [1, 1024]], nt * 524288 + oc * 131072),
                      SQ, waits=sw)
                srl = S.c(SQ)
                for i in range(8):
                    S.emit("vector", lambda e, i_=i: e.tensor_scalar(
                        uq[:, i_ * 128:(i_ + 1) * 128], qb[:, :], 4 * i_, 15,
                        Op.logical_shift_right, Op.bitwise_and),
                        waits=[(WQ, qzl)] if i == 0 else [], incs=[(VES, 1)])
                for i in range(8):
                    S.emit("vector", lambda e, i_=i: e.tensor_scalar(
                        uz[:, i_ * 128:(i_ + 1) * 128], zb[:, :], 4 * i_, 15,
                        Op.logical_shift_right, Op.bitwise_and), incs=[(VES, 1)])
                S.emit("vector", lambda e: e.tensor_tensor(uq[:, :], uq[:, :], uz[:, :],
                                                           Op.subtract), incs=[(VES, 1)])
                qz_ve = S.c(VES)
                in0 = ap_with(uq, [part_dim(uq), [16, 8], [1, 16], [128, 8]])
                S.emit("vector", lambda e, a=in0: e.tensor_tensor(
                    dqf[:, :], a, srb[:, :], Op.mult),
                    waits=[(VES, qz_ve), (SQ, srl), (SES, S.c(SES))], incs=[(VES, 1)])
                dq_ve = S.c(VES)
                col = oc * 1024
                S.emit("scalar", lambda e, c_=col, d=dsth: e.copy(
                    out=d[:, c_:c_ + 1024], in_=dqf[:, :]),
                    waits=[(VES, dq_ve)], incs=[(SES, 1)])
                hse = S.c(SES)
                S.emit("vector", lambda e, c_=col, d=dsth, dl=dstl: e.tensor_tensor(
                    dl[:, c_:c_ + 1024], dqf[:, :], d[:, c_:c_ + 1024], Op.subtract),
                    waits=[(SES, hse)], incs=[(VES, 1)])
            return S.c(VES), S.c(SES)

        hid_evac_ve, hid_evac_se, hid_aq = {}, {}, {}

        def gu_mms(nt, sset, acc0, acc1, rdy_ve, rdy_se, prev_evac):
            dsth, dstl = slotset(sset)
            for kt in range(NKT):
                first, last = (kt == 0), (kt == NKT - 1)
                wgh = dsth[:, kt * 128:(kt + 1) * 128]
                wgl = dstl[:, kt * 128:(kt + 1) * 128]
                hh = lambda mc_, k_=kt: h_h[:, k_ * M + mc_ * 512: k_ * M + mc_ * 512 + 512]
                hl = lambda mc_, k_=kt: h_l[:, k_ * M + mc_ * 512: k_ * M + mc_ * 512 + 512]
                seq = [(wgh, hh(0), acc0, first, False), (wgh, hh(1), acc1, first, False),
                       (wgh, hl(0), acc0, False, False), (wgh, hl(1), acc1, False, False),
                       (wgl, hh(0), acc0, False, last), (wgl, hh(1), acc1, False, last)]
                for i_, (wv, hv, acc, st, sp) in enumerate(seq):
                    waits = []
                    if i_ == 0 and kt == 0:
                        waits = [(VES, rdy_ve), (SES, rdy_se)]
                        if prev_evac is not None:
                            waits += [(VES, prev_evac[0]), (SES, prev_evac[1])]
                    S.emit("tensor", lambda e, w_=wv, h2=hv, a_=acc, st_=st, sp_=sp: e.matmul(
                        a_[:, :], w_, h2, start=st_, stop=sp_), waits=waits, incs=[(MMS, 1)])
            return S.c(MMS)

        gu_done = {("g", -1): S.c(MMS), ("u", -1): S.c(MMS)}
        for nt in range(NNT):
            par = nt % 2
            g0, g1 = PS[0 + 4 * par], PS[1 + 4 * par]
            u0, u1 = PS[2 + 4 * par], PS[3 + 4 * par]
            gve, gse = deq_gu(qg_i, zg_i, sg_i, nt, 0, gu_done[("g", nt - 1)])
            pe = (hid_evac_ve[nt - 2], hid_evac_se[nt - 2]) if nt >= 2 else None
            gu_done[("g", nt)] = gu_mms(nt, 0, g0, g1, gve, gse, pe)
            uve, use_ = deq_gu(qu_i, zu_i, su_i, nt, 1, gu_done[("u", nt - 1)])
            gu_done[("u", nt)] = gu_mms(nt, 1, u0, u1, uve, use_, pe)
            umm = gu_done[("u", nt)]
            for mc, (gp, up_) in enumerate(((g0, u0), (g1, u1))):
                waits = [(MMS, umm), (VES, S.c(VES)), (AQ, S.c(AQ))]
                S.emit("scalar", lambda e, g_=gp: e.activation(
                    out=silt[:, :], in_=g_[:, :], func=AF.Silu),
                    waits=waits, incs=[(SES, 1)])
                sse = S.c(SES)
                S.emit("vector", lambda e, u2=up_: e.tensor_tensor(
                    hidf[:, :], silt[:, :], u2[:, :], Op.mult),
                    waits=[(SES, sse)], incs=[(VES, 1)])
                hfve = S.c(VES)
                S.emit("scalar", lambda e: e.copy(out=hidb_h[:, :], in_=hidf[:, :]),
                       waits=[(VES, hfve)], incs=[(SES, 1)])
                hbse = S.c(SES)
                S.emit("vector", lambda e: e.tensor_tensor(
                    hidb_l[:, :], hidf[:, :], hidb_h[:, :], Op.subtract),
                    waits=[(SES, hbse)], incs=[(VES, 1)])
                S.dma("gpsimd", hidh[nt, :, mc * 512:(mc + 1) * 512], hidb_h[:, :], AQ,
                      waits=[(SES, hbse)])
                S.dma("gpsimd", hidl[nt, :, mc * 512:(mc + 1) * 512], hidb_l[:, :], AQ,
                      waits=[(VES, S.c(VES))])
            hid_evac_ve[nt] = S.c(VES)
            hid_evac_se[nt] = S.c(SES)
            hid_aq[nt] = S.c(AQ)
        hid_all_aq = S.c(AQ)
        gu_all_mm = S.c(MMS)

        srb5 = scr[:, 0:1024]
        dqf5 = scr[:, 2048:2560]
        outb = [scr[:, 2560:3072], scr[:, 3072:3584]]
        x2ch4 = scr[:, 3584:4096]

        def deq_d(nt_list, dc, sset, prev_mm_done):
            dsth, dstl = slotset(sset)
            n8 = len(nt_list)
            for pr in range((n8 + 1) // 2):
                nts = nt_list[pr * 2:pr * 2 + 2]
                nn = len(nts)
                w_ = 64 * nn
                waits = [(MMS, prev_mm_done), (VES, S.c(VES))]
                S.dma("sync", qb[:, 0:w_],
                      ap_with(qd_i, [[64, 128], [8192, nn], [1, 64]],
                              dc * NNT * 8192 + nts[0] * 8192), WQ, waits=waits)
                S.dma("sync", zb[:, 0:w_],
                      ap_with(zd_i, [[64, 128], [8192, nn], [1, 64]],
                              dc * NNT * 8192 + nts[0] * 8192), WQ)
                qzl = S.c(WQ)
                S.dma("scalar", srb5[:, 0:512 * nn],
                      ap_with(sd_i, [[512, 128], [65536, nn], [1, 512]],
                              dc * NNT * 65536 + nts[0] * 65536), SQ,
                      waits=[(VES, S.c(VES))])
                srl = S.c(SQ)
                for i in range(8):
                    S.emit("vector", lambda e, i_=i, ww=w_: e.tensor_scalar(
                        uq[:, i_ * 128:i_ * 128 + ww], qb[:, 0:ww], 4 * i_, 15,
                        Op.logical_shift_right, Op.bitwise_and),
                        waits=[(WQ, qzl)] if i == 0 else [], incs=[(VES, 1)])
                for i in range(8):
                    S.emit("vector", lambda e, i_=i, ww=w_: e.tensor_scalar(
                        uz[:, i_ * 128:i_ * 128 + ww], zb[:, 0:ww], 4 * i_, 15,
                        Op.logical_shift_right, Op.bitwise_and), incs=[(VES, 1)])
                S.emit("vector", lambda e: e.tensor_tensor(uq[:, :], uq[:, :], uz[:, :],
                                                           Op.subtract), incs=[(VES, 1)])
                qz_ve = S.c(VES)
                for ns in range(nn):
                    nt_loc = pr * 2 + ns
                    in0 = ap_with(uq, [part_dim(uq), [1, 64], [128, 8]], ns * 64)
                    S.emit("vector", lambda e, a=in0, ns_=ns: e.tensor_tensor(
                        dqf5[:, :], a, srb5[:, ns_ * 512:(ns_ + 1) * 512], Op.mult),
                        waits=[(VES, qz_ve), (SQ, srl), (SES, S.c(SES))], incs=[(VES, 1)])
                    S.emit("vector", lambda e, d_=dc: e.tensor_tensor(
                        dqf5[:, :], dqf5[:, :], gb_b[:, d_ * 512:(d_ + 1) * 512], Op.mult),
                        incs=[(VES, 1)])
                    dq_ve = S.c(VES)
                    col = nt_loc * 512
                    S.emit("scalar", lambda e, c_=col, d=dsth: e.copy(
                        out=d[:, c_:c_ + 512], in_=dqf5[:, :]),
                        waits=[(VES, dq_ve)], incs=[(SES, 1)])
                    hse = S.c(SES)
                    S.emit("vector", lambda e, c_=col, d=dsth, dl=dstl: e.tensor_tensor(
                        dl[:, c_:c_ + 512], dqf5[:, :], d[:, c_:c_ + 512], Op.subtract),
                        waits=[(SES, hse)], incs=[(VES, 1)])
            return S.c(VES), S.c(SES)

        dc_evac_se = {}
        out_aq_fin = 0
        oct_mm = {}
        x2tr_done = {}
        for dc in range(NDC):
            for mt in range(NMT):
                waits = []
                if dc == 0 and mt == 0:
                    waits = [(AQ, hid_all_aq), (MMS, gu_all_mm)]
                if mt > 0:
                    waits.append((MMS, x2tr_done[(dc, mt - 1)]))
                elif dc > 0:
                    waits.append((MMS, x2tr_done[(dc - 1, NMT - 1)]))
                if dc > 0:
                    waits.append((SES, dc_evac_se[(dc - 1, mt)]))
                    waits.append((AQ, S.c(AQ)))
                S.dma("gpsimd", x2ch4[:, :],
                      ap_with(x2s, [[M, 128], [128 * M, 4], [1, 128]],
                              (4 * dc) * 128 * M + mt * 128),
                      AQ, waits=waits)
                xld = S.c(AQ)
                for j in range(4):
                    waits = [(AQ, xld)] if j == 0 else []
                    if j == 0 and dc > 0:
                        waits.append((SES, dc_evac_se[(dc - 1, mt)]))
                    S.emit("tensor", lambda e, m_=mt, j_=j: e.matmul(
                        PS[m_][:, j_ * 128:(j_ + 1) * 128],
                        x2ch4[:, j_ * 128:(j_ + 1) * 128], ident[:, :],
                        is_transpose=True, start=(j_ == 0), stop=False),
                        waits=waits, incs=[(MMS, 1)])
                x2tr_done[(dc, mt)] = S.c(MMS)
            x2tr_mm = S.c(MMS)
            for oi, oct_ in enumerate(OCTS):
                gslot = oi % 2
                n8 = len(oct_)
                waits = [(MMS, x2tr_mm)] if oi < 2 else [(MMS, oct_mm[oi - 2])]
                hh_base = h_h[:, 0:1]
                hl_base = h_l[:, 0:1]
                S.dma("gpsimd",
                      ap_with(hh_base, [part_dim(hh_base), [1024, n8], [1, 1024]],
                              gslot * 8192),
                      ap_with(hidh, [[M, 128], [128 * M, n8], [1, M]], oct_[0] * 128 * M),
                      AQ, waits=waits)
                S.dma("gpsimd",
                      ap_with(hl_base, [part_dim(hl_base), [1024, n8], [1, 1024]],
                              gslot * 8192),
                      ap_with(hidl, [[M, 128], [128 * M, n8], [1, M]], oct_[0] * 128 * M),
                      AQ)
                hid_ld = S.c(AQ)
                dve_, dse_ = deq_d(oct_, dc, gslot,
                                   x2tr_mm if oi < 2 else oct_mm[oi - 2])
                dsth, dstl = slotset(gslot)
                for ni, nt in enumerate(oct_):
                    for mt in range(NMT):
                        off = gslot * 8192 + ni * 1024 + mt * 128
                        lhs_h = h_h[:, off:off + 128]
                        lhs_l = h_l[:, off:off + 128]
                        wdh = dsth[:, ni * 512:(ni + 1) * 512]
                        wdl = dstl[:, ni * 512:(ni + 1) * 512]
                        last = (nt == NNT - 1)
                        seq = [(lhs_h, wdh, False), (lhs_h, wdl, False), (lhs_l, wdh, last)]
                        for i_, (lh, wv, sp) in enumerate(seq):
                            waits = []
                            if ni == 0 and mt == 0 and i_ == 0:
                                waits = [(VES, dve_), (SES, dse_), (AQ, hid_ld)]
                            S.emit("tensor", lambda e, l_=lh, w_=wv, m_=mt, sp_=sp: e.matmul(
                                PS[m_][:, :], l_, w_, start=False, stop=sp_),
                                waits=waits, incs=[(MMS, 1)])
                oct_mm[oi] = S.c(MMS)
            dn_mm = S.c(MMS)
            for mt in range(NMT):
                slot = mt % 2
                waits = [(MMS, dn_mm)]
                if mt >= 2:
                    waits.append((AQ, S.c(AQ)))
                S.emit("scalar", lambda e, m_=mt, s_=slot: e.copy(
                    out=outb[s_][:, :], in_=PS[m_][:, :]),
                    waits=waits, incs=[(SES, 1)])
                dc_evac_se[(dc, mt)] = S.c(SES)
                S.dma("gpsimd", y_out[mt, :, dc * 512:(dc + 1) * 512], outb[slot][:, :],
                      AQ, waits=[(SES, S.c(SES))])
            out_aq_fin = S.c(AQ)

        S.emit("gpsimd", lambda e: e.wait_ge(d_aq, out_aq_fin))

        sems = {"d_wq": d_wq, "d_aq": d_aq, "d_sq": d_sq, "s_mm": s_mm,
                "s_ve": s_ve, "s_se": s_se, "s_cc": s_cc}

        def replay(engine, key):
            for fn, waits, incs in S.ev[key]:
                for s, v in waits:
                    engine.wait_ge(sems[s], v)
                ins = fn(engine)
                if ins is not None:
                    for s, amt in incs:
                        ins = ins.then_inc(sems[s], amt)

        @block.sync
        def _(eng):
            replay(eng, "sync")

        @block.gpsimd
        def _(eng):
            replay(eng, "gpsimd")

        @block.scalar
        def _(eng):
            replay(eng, "scalar")

        @block.vector
        def _(eng):
            replay(eng, "vector")

        @block.tensor
        def _(eng):
            replay(eng, "tensor")

    return nc


_NC_CACHE = None
_LAST_RES = None


def kernel(**inputs):
    global _NC_CACHE
    x = np.ascontiguousarray(np.asarray(inputs["L0_attn_data"], dtype=np.float32).reshape(8192, D))
    W = np.asarray(inputs["L0_attn_weight"], dtype=np.float32)
    Mk = np.asarray(inputs["L0_attn_mask"], dtype=np.int32)
    w_t = np.ascontiguousarray(W.reshape(NKT, 128, NKT, 128).transpose(0, 2, 1, 3))
    m_t = np.ascontiguousarray(Mk.reshape(NKT, 128, NKT, 128).transpose(0, 2, 1, 3))

    def tile_gu(q):
        return np.ascontiguousarray(q.reshape(4, 8, 128, NNT, 16).transpose(3, 0, 2, 1, 4))

    def tile_gu_s(s):
        return np.ascontiguousarray(s.reshape(4, 8, 128, NNT, 128).transpose(3, 0, 2, 1, 4))

    qg = tile_gu(np.asarray(inputs["L0_ffn_gate_qweight"], dtype=np.int32))
    qu = tile_gu(np.asarray(inputs["L0_ffn_up_qweight"], dtype=np.int32))
    zg = tile_gu(np.repeat(np.asarray(inputs["L0_ffn_gate_qzeros"], dtype=np.int32), 128, axis=0))
    zu = tile_gu(np.repeat(np.asarray(inputs["L0_ffn_up_qzeros"], dtype=np.int32), 128, axis=0))
    sg = tile_gu_s(np.repeat(np.asarray(inputs["L0_ffn_gate_scales"], dtype=np.float32), 128, axis=0))
    su = tile_gu_s(np.repeat(np.asarray(inputs["L0_ffn_up_scales"], dtype=np.float32), 128, axis=0))

    def tile_d(q, cw):
        return np.ascontiguousarray(q.reshape(NNT, 128, NDC, cw).transpose(2, 0, 1, 3))

    qd = tile_d(np.asarray(inputs["L0_ffn_down_qweight"], dtype=np.int32), 64)
    zd = tile_d(np.repeat(np.asarray(inputs["L0_ffn_down_qzeros"], dtype=np.int32), 128, axis=0), 64)
    sd = tile_d(np.repeat(np.asarray(inputs["L0_ffn_down_scales"], dtype=np.float32), 128, axis=0), 512)

    cv = np.ascontiguousarray(np.asarray(inputs["L0_c"], dtype=np.float32).reshape(1, D))
    wmod_full = np.asarray(inputs["L0_wmod"], dtype=np.float32)
    ones_a = np.ones((1, 128), dtype=np.float32)
    ident_a = np.eye(128, dtype=np.float32)

    if _NC_CACHE is None:
        _NC_CACHE = build_nc()
    nc = _NC_CACHE

    in_maps = []
    for c in range(NCORE):
        in_maps.append({
            "x0": np.ascontiguousarray(x[c * M:(c + 1) * M].reshape(NMT, 128, D)),
            "w_t": w_t, "m_t": m_t,
            "qg": qg, "qu": qu, "zg": zg, "zu": zu, "sg": sg, "su": su,
            "qd": qd, "zd": zd, "sd": sd,
            "cvec": cv,
            "wmod": np.ascontiguousarray(wmod_full[:, c * 3072:(c + 1) * 3072]),
            "ones_in": ones_a, "ident_in": ident_a,
        })
    res = run_bass_kernel_spmd(nc, in_maps, core_ids=list(range(NCORE)))
    global _LAST_RES
    _LAST_RES = res
    out = np.concatenate([r["y"].reshape(M, D) for r in res.results], axis=0)
    return out.reshape(1, 8192, D).astype(np.float32, copy=False)


# version 03
# speedup vs baseline: 1.0097x; 1.0097x over previous
import numpy as np
import ml_dtypes as mld
import concourse.bass as bass
import concourse.mybir as mybir
from concourse.bass_utils import run_bass_kernel_spmd
from concourse.alu_op_type import AluOpType as Op

F32 = mybir.dt.float32
BF16 = mybir.dt.bfloat16
I32 = mybir.dt.int32
AF = mybir.ActivationFunctionType

NCORE = 8
M = 1024
D = 4096
FI = 11008
NKT = D // 128
NNT = FI // 128
NMT = M // 128
NDC = D // 512
OCTS = [list(range(o * 8, min(o * 8 + 8, NNT))) for o in range((NNT + 7) // 8)]


class Sched:
    ENGINES = ("sync", "gpsimd", "scalar", "vector", "tensor")

    def __init__(self):
        self.ev = {e: [] for e in self.ENGINES}
        self.cnt = {}

    def emit(self, eng, fn, waits=(), incs=()):
        for s, v in waits:
            assert v <= self.cnt.get(s, 0), \
                f"[{eng}] wait {s}>={v} but serial count {self.cnt.get(s, 0)}"
        self.ev[eng].append((fn, [(s, v) for s, v in waits if v > 0], list(incs)))
        for s, amt in incs:
            self.cnt[s] = self.cnt.get(s, 0) + amt

    def dma(self, queue, out, in_, semname, waits=(), slow=False):
        self.emit(queue,
                  lambda e, o=out, i=in_, sl=slow: e.dma_start(
                      out=o, in_=i, allow_slow_non_contiguous=sl),
                  waits=waits, incs=[(semname, 16)])
        return self.cnt[semname]

    def c(self, s):
        return self.cnt.get(s, 0)


def ap_with(ap, dims, extra_offset=0):
    return bass.AP(tensor=ap.tensor, offset=ap.offset + extra_offset,
                   ap=[[int(d[0]), int(d[1])] for d in dims])


def part_dim(ap):
    return [int(ap.ap[0][0]), int(ap.ap[0][1])]


DEBUG = False


def build_nc():
    nc = bass.Bass("TRN2", target_bir_lowering=False)
    dbg = {"kind": "ExternalOutput"} if DEBUG else {}

    x_in = nc.dram_tensor("x0", [NMT, 128, D], F32, kind="ExternalInput").ap()
    w_t = nc.dram_tensor("w_t", [NKT, NKT, 128, 128], F32, kind="ExternalInput").ap()
    m_t = nc.dram_tensor("m_t", [NKT, NKT, 128, 128], I32, kind="ExternalInput").ap()
    qg_i = nc.dram_tensor("qg", [NNT, 4, 128, 8, 16], I32, kind="ExternalInput").ap()
    qu_i = nc.dram_tensor("qu", [NNT, 4, 128, 8, 16], I32, kind="ExternalInput").ap()
    zg_i = nc.dram_tensor("zg", [NNT, 4, 128, 8, 16], I32, kind="ExternalInput").ap()
    zu_i = nc.dram_tensor("zu", [NNT, 4, 128, 8, 16], I32, kind="ExternalInput").ap()
    sg_i = nc.dram_tensor("sg", [NNT, 4, 128, 8, 128], F32, kind="ExternalInput").ap()
    su_i = nc.dram_tensor("su", [NNT, 4, 128, 8, 128], F32, kind="ExternalInput").ap()
    qd_i = nc.dram_tensor("qd", [NDC, NNT, 128, 64], I32, kind="ExternalInput").ap()
    zd_i = nc.dram_tensor("zd", [NDC, NNT, 128, 64], I32, kind="ExternalInput").ap()
    sd_i = nc.dram_tensor("sd", [NDC, NNT, 128, 512], F32, kind="ExternalInput").ap()
    cvec = nc.dram_tensor("cvec", [1, D], F32, kind="ExternalInput").ap()
    wmod = nc.dram_tensor("wmod", [D, 3072], F32, kind="ExternalInput").ap()
    ones_in = nc.dram_tensor("ones_in", [1, 128], F32, kind="ExternalInput").ap()
    ident_in = nc.dram_tensor("ident_in", [128, 128], F32, kind="ExternalInput").ap()
    y_out = nc.dram_tensor("y", [NMT, 128, D], F32, kind="ExternalOutput").ap()

    mod_in_t = nc.dram_tensor("mod_in", [3072], F32)
    gbrow_t = nc.dram_tensor("gbrow", [4096], F32)
    mod_out_t = nc.dram_tensor("mod_out", [6 * D], F32)
    mod_dbg_t = nc.dram_tensor("mod_dbg", [6 * D], F32, **dbg) if DEBUG else None
    csi_dbg_t = nc.dram_tensor("csi_dbg", [128, NKT], F32, **dbg) if DEBUG else None
    modin_dbg_t = nc.dram_tensor("modin_dbg", [1, 3072], F32, **dbg) if DEBUG else None
    vecs_dbg_t = nc.dram_tensor("vecs_dbg", [128, 6 * NKT], F32, **dbg) if DEBUG else None
    h1h_dbg_t = nc.dram_tensor("h1h_dbg", [128, NKT * M], BF16, **dbg) if DEBUG else None
    h1l_dbg_t = nc.dram_tensor("h1l_dbg", [128, NKT * M], BF16, **dbg) if DEBUG else None
    wmf_dbg_t = nc.dram_tensor("wmf_dbg", [128, 128], F32, **dbg) if DEBUG else None
    h2h_dbg_t = nc.dram_tensor("h2h_dbg", [128, NKT * M], BF16, **dbg) if DEBUG else None
    h2l_dbg_t = nc.dram_tensor("h2l_dbg", [128, NKT * M], BF16, **dbg) if DEBUG else None
    ln2_dbg_t = nc.dram_tensor("ln2_dbg", [128, 2048], F32, **dbg) if DEBUG else None
    wmth_dbg_t = nc.dram_tensor("wmth_dbg", [128, 128], BF16, **dbg) if DEBUG else None
    wmtl_dbg_t = nc.dram_tensor("wmtl_dbg", [128, 128], BF16, **dbg) if DEBUG else None
    x1T_t = nc.dram_tensor("x1T_s", [NKT, 128, M], F32, **dbg)
    x2_t = nc.dram_tensor("x2_s", [NKT, 128, M], F32, **dbg)
    hidh_t = nc.dram_tensor("hidh_s", [NNT, 128, M], BF16, **dbg)
    hidl_t = nc.dram_tensor("hidl_s", [NNT, 128, M], BF16, **dbg)
    mod_in, mod_out = mod_in_t.ap(), mod_out_t.ap()
    gbrow = gbrow_t.ap()
    x1T, x2s, hidh, hidl = x1T_t.ap(), x2_t.ap(), hidh_t.ap(), hidl_t.ap()

    S = Sched()
    WQ, AQ, SQ, MMS, VES, SES, CCS = "d_wq", "d_aq", "d_sq", "s_mm", "s_ve", "s_se", "s_cc"

    import contextlib
    stack = contextlib.ExitStack()
    with stack:
        ent = stack.enter_context
        h_h = ent(nc.sbuf_tensor("h_h", [128, NKT * M], BF16))
        h_l = ent(nc.sbuf_tensor("h_l", [128, NKT * M], BF16))
        scr = ent(nc.sbuf_tensor("scr", [128, 4096], F32))
        bct = ent(nc.sbuf_tensor("bct", [128, 4096], F32))
        ring = ent(nc.sbuf_tensor("ring", [128, 17408], BF16))
        iscr = ent(nc.sbuf_tensor("iscr", [128, 2304], I32))
        mod_km = ent(nc.sbuf_tensor("mod_km", [128, 6 * NKT], F32))
        c_km = ent(nc.sbuf_tensor("c_km", [128, NKT], F32))
        c_si = ent(nc.sbuf_tensor("c_si", [128, NKT], F32))
        vecs = ent(nc.sbuf_tensor("vecs", [128, 6 * NKT], F32))
        ones_t = ent(nc.sbuf_tensor("ones_t", [1, 128], F32))
        ones_c = ent(nc.sbuf_tensor("ones_c", [128, 1], F32))
        ident = ent(nc.sbuf_tensor("ident", [128, 128], F32))
        eps_t = ent(nc.sbuf_tensor("eps_t", [128, 1], F32))
        ps0 = ent(nc.psum_tensor("ps0", [128, 512], F32))
        ps1 = ent(nc.psum_tensor("ps1", [128, 512], F32))
        ps2 = ent(nc.psum_tensor("ps2", [128, 512], F32))
        ps3 = ent(nc.psum_tensor("ps3", [128, 512], F32))
        ps4 = ent(nc.psum_tensor("ps4", [128, 512], F32))
        ps5 = ent(nc.psum_tensor("ps5", [128, 512], F32))
        ps6 = ent(nc.psum_tensor("ps6", [128, 512], F32))
        ps7 = ent(nc.psum_tensor("ps7", [128, 512], F32))
        chan_sems = {n: ent(nc.semaphore(n)) for n in [
            "c_const", "c_ckm", "c_wm0", "c_wm1", "c_misc", "c_misc2", "c_xa",
            "c_x1t0", "c_x1t1", "c_x1c", "c_x2st0", "c_x2st1", "c_ap0", "c_ap1",
            "c_wn0", "c_wn1", "c_wn2", "c_wn3", "c_qz", "c_srb", "c_hs",
            "c_ho0", "c_ho1", "c_x2c", "c_ob0", "c_ob1", "c_dbg"]}
        s_mm = ent(nc.semaphore("s_mm"))
        s_ve = ent(nc.semaphore("s_ve"))
        s_se = ent(nc.semaphore("s_se"))
        s_cc = ent(nc.semaphore("s_cc"))
        block = ent(nc.Block())
        PS = [ps0, ps1, ps2, ps3, ps4, ps5, ps6, ps7]

        CONST = "c_const"
        S.dma("sync", ones_t[:, :], ones_in[:, :], CONST)
        S.dma("sync", ident[:, :], ident_in[:, :], CONST)
        S.emit("vector", lambda e: e.memset(ones_c[:, :], 1.0), incs=[(VES, 1)])
        S.emit("vector", lambda e: e.memset(eps_t[:, :], 1e-5), incs=[(VES, 1)])
        const_wq = S.c(CONST)

        S.dma("gpsimd", c_km[:, :], ap_with(cvec, [[1, 128], [128, NKT]]), "c_ckm", slow=True)
        S.emit("scalar", lambda e: e.activation(out=c_si[:, :], in_=c_km[:, :], func=AF.Silu),
               waits=[("c_ckm", 16)], incs=[(SES, 1)])
        silu_se = S.c(SES)

        wmbuf = [scr[:, 0:512], scr[:, 512:1024]]
        mm_base = S.c(MMS)
        for ch in range(6):
            for kt in range(NKT):
                idx = ch * NKT + kt
                wchan = "c_wm%d" % (idx % 2)
                waits = [(CONST, const_wq)] if idx == 0 else []
                if idx >= 2:
                    waits.append((MMS, mm_base + idx - 1))
                wl = S.dma("sync", wmbuf[idx % 2][:, :],
                           ap_with(wmod, [[3072, 128], [1, 512]], kt * 128 * 3072 + ch * 512),
                           wchan, waits=waits)
                mmw = [(wchan, wl)]
                if idx == 0:
                    mmw.append((SES, silu_se))
                S.emit("tensor",
                       lambda e, c_=ch, k_=kt, s_=idx % 2: e.matmul(
                           PS[c_][0:1, :], c_si[:, k_:k_ + 1], wmbuf[s_][:, :],
                           start=(k_ == 0), stop=(k_ == NKT - 1)),
                       waits=mmw, incs=[(MMS, 1)])

        modsl = bct[0:1, 0:3072]
        for ch in range(6):
            S.emit("vector",
                   lambda e, c_=ch: e.tensor_copy(modsl[:, c_ * 512:(c_ + 1) * 512],
                                                  PS[c_][0:1, :]),
                   waits=[(MMS, mm_base + ch * NKT + NKT)], incs=[(VES, 1)])
        p0_ve = S.c(VES)
        if DEBUG:
            S.dma("gpsimd", csi_dbg_t.ap(), c_si[:, :], "c_dbg", waits=[(SES, silu_se)])
            S.dma("gpsimd", modin_dbg_t.ap(), modsl[:, :], "c_dbg", waits=[(VES, p0_ve)])
        S.dma("gpsimd", ap_with(mod_in, [[3072, 1], [1, 3072]]), modsl[:, :],
              "c_misc", waits=[(VES, p0_ve)])
        S.emit("gpsimd",
               lambda e: e.collective_compute(
                   "AllGather", Op.bypass, replica_groups=[list(range(NCORE))],
                   ins=[mod_in_t.ap().opt()], outs=[mod_out_t.ap().opt()]),
               waits=[("c_misc", S.c("c_misc"))], incs=[(CCS, 1)])
        mtmp = [scr[:, 0:128], scr[:, 128:256]]
        mod_tr_se = 0
        for hvi in range(2):
            mchan = "c_misc" if hvi == 0 else "c_misc2"
            S.dma("gpsimd", mtmp[hvi][0:96, :],
                  ap_with(mod_out, [[128, 96], [1, 128]], hvi * 12288), mchan,
                  waits=[(CCS, 1), (MMS, mm_base + 6 * NKT)])
            mld = S.c(mchan)
            S.emit("tensor", lambda e, h_=hvi: e.matmul(
                PS[6][:, 0:96], mtmp[h_][0:96, :], ident[0:96, 0:96],
                is_transpose=True, start=True, stop=True),
                waits=[(mchan, mld), (CONST, const_wq)] +
                      ([(SES, mod_tr_se)] if hvi else []), incs=[(MMS, 1)])
            S.emit("scalar", lambda e, h_=hvi: e.copy(
                out=mod_km[:, h_ * 96:(h_ + 1) * 96], in_=PS[6][:, 0:96]),
                waits=[(MMS, S.c(MMS))], incs=[(SES, 1)])
            mod_tr_se = S.c(SES)
        mod_se = S.c(SES)

        K = NKT
        def drv(fn):
            S.emit("vector", fn, waits=[(SES, mod_se)], incs=[(VES, 1)])
        drv(lambda e: e.tensor_copy(vecs[:, 0:K], mod_km[:, 0:K]))
        drv(lambda e: e.tensor_scalar_add(vecs[:, K:2 * K], mod_km[:, K:2 * K], 1.0))
        drv(lambda e: e.tensor_copy(vecs[:, 3 * K:4 * K], mod_km[:, 3 * K:4 * K]))
        drv(lambda e: e.tensor_scalar_add(vecs[:, 4 * K:5 * K], mod_km[:, 4 * K:5 * K], 1.0))
        for off in (2 * K, 5 * K):
            g_ap = mod_km[:, off:off + K]
            o_ap = vecs[:, off:off + K]
            t1 = scr[:, 1024:1024 + K]
            t2 = scr[:, 1024 + K:1024 + 2 * K]
            mk = scr[:, 1024 + 2 * K:1024 + 3 * K]
            drv(lambda e, g=g_ap, t=t1: e.tensor_scalar_max(t, g, 0.25))
            drv(lambda e, g=g_ap, t=t2: e.tensor_scalar_min(t, g, -0.25))
            drv(lambda e, g=g_ap, m_=mk: e.tensor_scalar(m_, g, 0.0, None, Op.is_ge))
            drv(lambda e, a=t1, b=t2, o=o_ap: e.tensor_tensor(o, a, b, Op.subtract))
            drv(lambda e, m_=mk, o=o_ap: e.tensor_tensor(o, o, m_, Op.mult))
            drv(lambda e, b=t2, o=o_ap: e.tensor_tensor(o, o, b, Op.add))
        vec_ve = S.c(VES)
        if DEBUG:
            S.dma("gpsimd", vecs_dbg_t.ap(), vecs[:, :], "c_dbg", waits=[(VES, vec_ve)])
        sa_v = vecs[:, 0:K]; ca1_v = vecs[:, K:2 * K]; ga_v = vecs[:, 2 * K:3 * K]
        sb_v = vecs[:, 3 * K:4 * K]; cb1_v = vecs[:, 4 * K:5 * K]; gb_v = vecs[:, 5 * K:6 * K]

        def bcast_row(row_ap, dst, width, extra_wait=()):
            nch = (width + 511) // 512
            for chi in range(nch):
                w_ = min(512, width - chi * 512)
                b_ = PS[4 + (chi % 2)]
                waits = list(extra_wait) if chi == 0 else []
                if chi >= 2:
                    waits.append((SES, S.c(SES) - 1))
                S.emit("tensor",
                       lambda e, r=row_ap, c_=chi, ww=w_, b2=b_: e.matmul(
                           b2[:, 0:ww], ones_t[:, :], r[:, c_ * 512:c_ * 512 + ww],
                           start=True, stop=True),
                       waits=waits, incs=[(MMS, 1)])
                S.emit("scalar",
                       lambda e, c_=chi, ww=w_, b2=b_, d=dst: e.copy(
                           out=d[:, c_ * 512:c_ * 512 + ww], in_=b2[:, 0:ww]),
                       waits=[(MMS, S.c(MMS))], incs=[(SES, 1)])

        xa = scr[:, 0:2048]
        xch = [scr[:, 2048:2176], scr[:, 2176:2304]]
        sqc = [scr[:, 2304:2432], scr[:, 2432:2560]]
        evac_se = {}
        statx_mm = {}
        statq_mm = {}
        x1t_aq = {}
        for mt in range(NMT):
            xl = 0
            for kt in range(NKT):
                gi = mt * NKT + kt
                if kt % 16 == 0:
                    if gi == 0:
                        waits = [(VES, vec_ve)]
                    else:
                        waits = [(SES, evac_se[gi - 1])]
                    S.dma("gpsimd", xa[:, :],
                          ap_with(x_in, [[D, 128], [1, 2048]],
                                  mt * 128 * D + (kt // 16) * 2048),
                          "c_xa", waits=waits)
                    xl = S.c("c_xa")
                b_ = PS[6 + (gi % 2)]
                slot = gi % 2
                waits = [("c_xa", xl)]
                if gi >= 2:
                    waits.append((SES, evac_se[gi - 2]))
                S.emit("tensor",
                       lambda e, k_=kt, b2=b_: e.matmul(
                           b2[:, 0:128], xa[:, (k_ % 16) * 128:(k_ % 16) * 128 + 128],
                           ident[:, :], is_transpose=True, start=True, stop=True),
                       waits=waits, incs=[(MMS, 1)])
                trmm = S.c(MMS)
                waits = [(MMS, trmm)]
                if gi >= 2:
                    waits += [(MMS, statx_mm[gi - 2]),
                              ("c_x1t%d" % (gi % 2), x1t_aq[gi - 2])]
                S.emit("scalar",
                       lambda e, s_=slot, b2=b_: e.copy(out=xch[s_][:, :], in_=b2[:, 0:128]),
                       waits=waits, incs=[(SES, 1)])
                evac_se[gi] = S.c(SES)
                waits = []
                if gi >= 2:
                    waits.append((MMS, statq_mm[gi - 2]))
                S.emit("scalar",
                       lambda e, s_=slot: e.activation(out=sqc[s_][:, :], in_=xch[s_][:, :],
                                                       func=AF.Square),
                       waits=waits, incs=[(SES, 1)])
                sqe = S.c(SES)
                half, mcol = mt // 4, (mt % 4) * 128
                fi_ = (kt == 0 and mt % 4 == 0)
                la_ = (kt == NKT - 1 and mt % 4 == 3)
                waits = [(SES, evac_se[gi])]
                if gi == 0:
                    waits.append((VES, p0_ve))
                S.emit("tensor",
                       lambda e, s_=slot, h_=half, mc=mcol, f=fi_, l=la_: e.matmul(
                           PS[h_][0:1, mc:mc + 128], ones_c[:, :], xch[s_][:, :],
                           start=f, stop=l),
                       waits=waits, incs=[(MMS, 1)])
                statx_mm[gi] = S.c(MMS)
                S.emit("tensor",
                       lambda e, s_=slot, h_=half, mc=mcol, f=fi_, l=la_: e.matmul(
                           PS[2 + h_][0:1, mc:mc + 128], ones_c[:, :], sqc[s_][:, :],
                           start=f, stop=l),
                       waits=[(SES, sqe)], incs=[(MMS, 1)])
                statq_mm[gi] = S.c(MMS)
                xchan = "c_x1t%d" % (gi % 2)
                S.dma("gpsimd", x1T[kt, :, mt * 128:(mt + 1) * 128], xch[slot][:, :],
                      xchan, waits=[(SES, evac_se[gi])])
                x1t_aq[gi] = S.c(xchan)
        stats1_mm = S.c(MMS)
        x1T_aq_all = [("c_x1t0", S.c("c_x1t0")), ("c_x1t1", S.c("c_x1t1"))]

        def ln_rows(wait_mm):
            mean = bct[0:1, 0:1024]
            var = bct[0:1, 1024:2048]
            rstd = bct[0:1, 2048:3072]
            nmrs = bct[0:1, 3072:4096]
            S.emit("vector", lambda e: e.tensor_copy(mean[:, 0:512], PS[0][0:1, :]),
                   waits=[(MMS, wait_mm)], incs=[(VES, 1)])
            S.emit("vector", lambda e: e.tensor_copy(mean[:, 512:1024], PS[1][0:1, :]), incs=[(VES, 1)])
            S.emit("vector", lambda e: e.tensor_copy(var[:, 0:512], PS[2][0:1, :]), incs=[(VES, 1)])
            S.emit("vector", lambda e: e.tensor_copy(var[:, 512:1024], PS[3][0:1, :]), incs=[(VES, 1)])
            S.emit("vector", lambda e: e.tensor_scalar_mul(mean[:, :], mean[:, :], 1.0 / D), incs=[(VES, 1)])
            S.emit("vector", lambda e: e.tensor_scalar_mul(var[:, :], var[:, :], 1.0 / D), incs=[(VES, 1)])
            S.emit("vector", lambda e: e.tensor_tensor(rstd[:, :], mean[:, :], mean[:, :], Op.mult),
                   incs=[(VES, 1)])
            S.emit("vector", lambda e: e.tensor_tensor(var[:, :], var[:, :], rstd[:, :], Op.subtract),
                   incs=[(VES, 1)])
            vv = S.c(VES)
            S.emit("scalar", lambda e: e.activation(out=rstd[:, :], in_=var[:, :], func=AF.Sqrt,
                                                    bias=eps_t[0:1, :]),
                   waits=[(VES, vv)], incs=[(SES, 1)])
            S.emit("vector", lambda e: e.reciprocal(rstd[:, :], rstd[:, :]),
                   waits=[(SES, S.c(SES))], incs=[(VES, 1)])
            S.emit("vector", lambda e: e.scalar_tensor_tensor(nmrs[:, :], mean[:, :], -1.0, rstd[:, :],
                                                              Op.mult, Op.mult), incs=[(VES, 1)])
            return rstd, nmrs, S.c(VES)

        rstd_b = bct[:, 0:1024]
        nmrs_b = bct[:, 1024:2048]

        def apply_pass(xT_src, ca1_col, sa_col, first_wait):
            bufs = [scr[:, 0:1024], scr[:, 1024:2048]]
            ht = scr[:, 2048:3072]
            buf_free = {}
            for kt in range(NKT):
                slot = kt % 2
                achan = "c_ap%d" % slot
                waits = list(first_wait) if kt == 0 else []
                if kt >= 2:
                    waits.append((VES, buf_free[kt - 2]))
                S.dma("gpsimd", bufs[slot][:, :], xT_src[kt], achan, waits=waits)
                ld = S.c(achan)
                S.emit("vector", lambda e, s_=slot: e.tensor_tensor(
                    bufs[s_][:, :], bufs[s_][:, :], rstd_b[:, :], Op.mult),
                    waits=[(achan, ld)], incs=[(VES, 1)])
                S.emit("vector", lambda e, s_=slot: e.tensor_tensor(
                    ht[:, :], bufs[s_][:, :], nmrs_b[:, :], Op.add),
                    waits=[(VES, S.c(VES))], incs=[(VES, 1)])
                buf_free[kt] = S.c(VES)
                S.emit("scalar", lambda e, k_=kt, cc=ca1_col, ss=sa_col: e.activation(
                    out=ht[:, :], in_=ht[:, :], func=AF.Identity,
                    scale=cc[:, k_:k_ + 1], bias=ss[:, k_:k_ + 1]),
                    waits=[(VES, buf_free[kt])], incs=[(SES, 1)])
                S.emit("scalar", lambda e, k_=kt: e.copy(
                    out=h_h[:, k_ * M:(k_ + 1) * M], in_=ht[:, :]), incs=[(SES, 1)])
                hse = S.c(SES)
                S.emit("vector", lambda e, k_=kt: e.tensor_tensor(
                    h_l[:, k_ * M:(k_ + 1) * M], ht[:, :], h_h[:, k_ * M:(k_ + 1) * M],
                    Op.subtract),
                    waits=[(SES, hse)], incs=[(VES, 1)])
            return S.c(VES), S.c(SES)

        r1, n1, lnv1 = ln_rows(stats1_mm)
        bcast_row(r1, rstd_b, 1024, extra_wait=[(VES, lnv1)])
        bcast_row(n1, nmrs_b, 1024)
        h1_ve, h1_se = apply_pass(x1T, ca1_v, sa_v,
                                  [(SES, S.c(SES))] + x1T_aq_all)
        if DEBUG:
            S.dma("gpsimd", h1h_dbg_t.ap(), h_h[:, :], "c_dbg", waits=[(VES, h1_ve), (SES, h1_se)])
            S.dma("gpsimd", h1l_dbg_t.ap(), h_l[:, :], "c_dbg")

        wnat = [scr[:, 3072:3200], scr[:, 3200:3328],
                scr[:, 3328:3456], scr[:, 3456:3584]]
        mnat = [iscr[:, 0:128], iscr[:, 128:256],
                iscr[:, 256:384], iscr[:, 384:512]]
        wm_f = [scr[:, 3584:3712], scr[:, 3712:3840]]
        wmt_h = [ring[:, i * 128:(i + 1) * 128] for i in range(3)]
        wmt_l = [ring[:, (3 + i) * 128:(4 + i) * 128] for i in range(3)]
        x1c = [scr[:, 0:512], scr[:, 512:1024]]
        x2c = [scr[:, 1024:1536], scr[:, 1536:2048]]

        tt_ve, tr_mm, hi_se, lo_ve, accmm = {}, {}, {}, {}, {}
        x2st_aq = {}
        dbg_hold = {}
        epi_ve = {}
        for nt in range(NKT):
            accA = PS[0 + 2 * (nt % 2)]
            accB = PS[1 + 2 * (nt % 2)]
            for mc in range(2):
                waits = []
                if nt == 0 and mc == 0:
                    waits.append((VES, h1_ve))
                if nt >= 1:
                    waits.append((VES, epi_ve[nt - 1]))
                S.dma("gpsimd", x1c[mc][:, :], x1T[nt, :, mc * 512:(mc + 1) * 512], "c_x1c",
                      waits=waits)
            x1_ld = S.c("c_x1c")
            for kt in range(NKT):
                gi = nt * NKT + kt
                s3, s2 = gi % 3, gi % 2
                s4 = gi % 4
                waits = []
                if gi == 0:
                    waits.append((MMS, stats1_mm))
                if gi >= 4:
                    waits.append((VES, tt_ve[gi - 4]))
                wchan = "c_wn%d" % s4
                S.dma("sync", wnat[s4][:, :], w_t[nt, kt], wchan, waits=waits)
                S.dma("sync", mnat[s4][:, :], m_t[nt, kt], wchan)
                wl = S.c(wchan)
                waits = [(wchan, wl)]
                if gi >= 2:
                    waits.append((MMS, tr_mm[gi - 2]))
                if DEBUG and gi == 2 and 0 in dbg_hold:
                    waits.append(("c_dbg", dbg_hold[0]))
                S.emit("vector", lambda e, s_=s2, s4_=s4: e.tensor_tensor(
                    wm_f[s_][:, :], wnat[s4_][:, :], mnat[s4_][:, :], Op.mult),
                    waits=waits, incs=[(VES, 1)])
                tt_ve[gi] = S.c(VES)
                trb = PS[6 + (gi % 2)]
                waits = [(VES, tt_ve[gi])]
                if gi == 0:
                    waits.append((SES, h1_se))
                if gi >= 2:
                    waits += [(SES, hi_se[gi - 2]), (VES, lo_ve[gi - 2])]
                S.emit("tensor", lambda e, s_=s2, b2=trb: e.matmul(
                    b2[:, 0:128], wm_f[s_][:, :], ident[:, :], is_transpose=True,
                    start=True, stop=True),
                    waits=waits, incs=[(MMS, 1)])
                tr_mm[gi] = S.c(MMS)
                waits = [(MMS, tr_mm[gi])]
                if gi >= 3:
                    waits.append((MMS, accmm[gi - 3]))
                if DEBUG and gi == 3 and 0 in dbg_hold:
                    waits.append(("c_dbg", dbg_hold[0]))
                S.emit("scalar", lambda e, s_=s3, b2=trb: e.copy(
                    out=wmt_h[s_][:, :], in_=b2[:, 0:128]),
                    waits=waits, incs=[(SES, 1)])
                hi_se[gi] = S.c(SES)
                S.emit("vector", lambda e, s_=s3, b2=trb: e.tensor_tensor(
                    wmt_l[s_][:, :], b2[:, 0:128], wmt_h[s_][:, :], Op.subtract),
                    waits=[(SES, hi_se[gi])], incs=[(VES, 1)])
                lo_ve[gi] = S.c(VES)
                if DEBUG and gi == 0:
                    S.dma("gpsimd", wmf_dbg_t.ap(), wm_f[0][:, :], "c_dbg",
                          waits=[(VES, tt_ve[0])])
                    S.dma("gpsimd", wmth_dbg_t.ap(), wmt_h[0][:, :], "c_dbg",
                          waits=[(SES, hi_se[0])])
                    S.dma("gpsimd", wmtl_dbg_t.ap(), wmt_l[0][:, :], "c_dbg",
                          waits=[(VES, lo_ve[0])])
                    dbg_hold[0] = S.c("c_dbg")
                hh = lambda mc_, k_=kt: h_h[:, k_ * M + mc_ * 512: k_ * M + mc_ * 512 + 512]
                hl = lambda mc_, k_=kt: h_l[:, k_ * M + mc_ * 512: k_ * M + mc_ * 512 + 512]
                first, last = (kt == 0), (kt == NKT - 1)
                seq = [(wmt_h[s3], hh(0), accA, first, False),
                       (wmt_h[s3], hh(1), accB, first, False),
                       (wmt_h[s3], hl(0), accA, False, False),
                       (wmt_h[s3], hl(1), accB, False, False),
                       (wmt_l[s3], hh(0), accA, False, last),
                       (wmt_l[s3], hh(1), accB, False, last)]
                for i_, (wv, hv, acc, st, sp) in enumerate(seq):
                    waits = []
                    if i_ == 0:
                        waits = [(SES, hi_se[gi])]
                        if kt == 0 and nt >= 2:
                            waits.append((VES, epi_ve[nt - 2]))
                    if i_ == 4:
                        waits = [(VES, lo_ve[gi])]
                    S.emit("tensor", lambda e, w_=wv, h2=hv, a_=acc, st_=st, sp_=sp: e.matmul(
                        a_[:, :], w_, h2, start=st_, stop=sp_),
                        waits=waits, incs=[(MMS, 1)])
                accmm[gi] = S.c(MMS)
            for mc, acc in ((0, accA), (1, accB)):
                waits = [(MMS, accmm[nt * NKT + NKT - 1]), ("c_x1c", x1_ld)]
                if nt >= 1:
                    waits.append(("c_x2st%d" % ((nt - 1) % 2), x2st_aq[nt - 1]))
                S.emit("vector", lambda e, n_=nt, a_=acc, mc_=mc: e.scalar_tensor_tensor(
                    x2c[mc_][:, :], a_[:, :], ga_v[:, n_:n_ + 1], x1c[mc_][:, :],
                    Op.mult, Op.add),
                    waits=waits,
                    incs=[(VES, 1)])
                S.dma("gpsimd", x2s[nt, :, mc * 512:(mc + 1) * 512], x2c[mc][:, :],
                      "c_x2st%d" % (nt % 2), waits=[(VES, S.c(VES))])
            epi_ve[nt] = S.c(VES)
            x2st_aq[nt] = S.c("c_x2st%d" % (nt % 2))
        x2_aq = [("c_x2st0", S.c("c_x2st0")), ("c_x2st1", S.c("c_x2st1"))]
        attn_mm = S.c(MMS)

        bufs3 = [scr[:, 0:1024], scr[:, 1024:2048]]
        sqt = scr[:, 2048:3072]
        last_stat = {}
        for kt in range(NKT):
            slot = kt % 2
            achan = "c_ap%d" % slot
            waits = list(x2_aq) + [(MMS, attn_mm)] if kt == 0 else []
            if kt >= 2:
                waits.append((MMS, last_stat[kt - 2]))
            S.dma("gpsimd", bufs3[slot][:, :], x2s[kt], achan, waits=waits)
            ld = S.c(achan)
            sq_w = [(achan, ld)]
            if kt >= 1:
                sq_w.append((MMS, last_stat[kt - 1]))
            S.emit("scalar", lambda e, s_=slot: e.activation(out=sqt[:, :], in_=bufs3[s_][:, :],
                                                             func=AF.Square),
                   waits=sq_w, incs=[(SES, 1)])
            sqse = S.c(SES)
            for mc in range(2):
                waits = [(achan, ld)]
                if kt == 0 and mc == 0:
                    waits.append((VES, lnv1))
                S.emit("tensor", lambda e, s_=slot, mc_=mc, k_=kt: e.matmul(
                    PS[mc_][0:1, :], ones_c[:, :], bufs3[s_][:, mc_ * 512:(mc_ + 1) * 512],
                    start=(k_ == 0), stop=(k_ == NKT - 1)),
                    waits=waits, incs=[(MMS, 1)])
                S.emit("tensor", lambda e, mc_=mc, k_=kt: e.matmul(
                    PS[2 + mc_][0:1, :], ones_c[:, :], sqt[:, mc_ * 512:(mc_ + 1) * 512],
                    start=(k_ == 0), stop=(k_ == NKT - 1)),
                    waits=[(SES, sqse)], incs=[(MMS, 1)])
            last_stat[kt] = S.c(MMS)
        stats2_mm = S.c(MMS)
        r2, n2, lnv2 = ln_rows(stats2_mm)
        bcast_row(r2, rstd_b, 1024, extra_wait=[(VES, lnv2)])
        bcast_row(n2, nmrs_b, 1024)
        if DEBUG:
            S.dma("gpsimd", ln2_dbg_t.ap(), bct[:, 0:2048], "c_dbg",
                  waits=[(SES, S.c(SES)), (VES, S.c(VES))])
        h2_ve, h2_se = apply_pass(x2s, cb1_v, sb_v, [(SES, S.c(SES))])
        if DEBUG:
            S.dma("gpsimd", h2h_dbg_t.ap(), h_h[:, :], "c_dbg", waits=[(VES, h2_ve), (SES, h2_se)])
            S.dma("gpsimd", h2l_dbg_t.ap(), h_l[:, :], "c_dbg")

        gb_row2 = scr[0:1, 0:4096]
        S.dma("gpsimd", ap_with(gbrow, [[1, 128], [128, NKT]]), gb_v[:, :], "c_misc",
              waits=[(VES, h2_ve), (SES, h2_se)], slow=True)
        S.dma("gpsimd", gb_row2[:, :], ap_with(gbrow, [[4096, 1], [1, 4096]]), "c_misc",
              waits=[("c_misc", S.c("c_misc"))])
        gb_aq = S.c("c_misc")
        gb_b = bct[:, 0:4096]
        bcast_row(gb_row2, gb_b, 4096, extra_wait=[("c_misc", gb_aq)])
        gb_se = S.c(SES)
        gb_mm = S.c(MMS)

        def slotset(si):
            return (ring[:, si * 8192: si * 8192 + 4096],
                    ring[:, si * 8192 + 4096: si * 8192 + 8192])
        hidb_h = ring[:, 16384:16896]
        hidb_l = ring[:, 16896:17408]
        qb = iscr[:, 0:128]
        zb = iscr[:, 128:256]
        uq = iscr[:, 256:1280]
        uz = iscr[:, 1280:2304]
        srb = scr[:, 0:1024]
        dqf = scr[:, 2048:3072]
        silt = scr[:, 3072:3584]
        hidf = scr[:, 3584:4096]

        first_p4 = True

        def deq_gu(q_i, z_i, s_i, nt, sset, prev_mm_done):
            nonlocal first_p4
            dsth, dstl = slotset(sset)
            for oc in range(4):
                waits = [(MMS, prev_mm_done), (VES, S.c(VES))]
                if first_p4:
                    waits.append((SES, gb_se))
                S.dma("sync", qb[:, :],
                      ap_with(q_i, [[128, 128], [1, 128]], nt * 65536 + oc * 16384),
                      "c_qz", waits=waits)
                S.dma("sync", zb[:, :],
                      ap_with(z_i, [[128, 128], [1, 128]], nt * 65536 + oc * 16384), "c_qz")
                qzl = S.c("c_qz")
                sw = [(VES, S.c(VES))]
                if first_p4:
                    sw += [(SES, gb_se), (MMS, gb_mm)]
                    first_p4 = False
                S.dma("scalar", srb[:, :],
                      ap_with(s_i, [[1024, 128], [1, 1024]], nt * 524288 + oc * 131072),
                      "c_srb", waits=sw)
                srl = S.c("c_srb")
                for i in range(8):
                    S.emit("vector", lambda e, i_=i: e.tensor_scalar(
                        uq[:, i_ * 128:(i_ + 1) * 128], qb[:, :], 4 * i_, 15,
                        Op.logical_shift_right, Op.bitwise_and),
                        waits=[("c_qz", qzl)] if i == 0 else [], incs=[(VES, 1)])
                for i in range(8):
                    S.emit("vector", lambda e, i_=i: e.tensor_scalar(
                        uz[:, i_ * 128:(i_ + 1) * 128], zb[:, :], 4 * i_, 15,
                        Op.logical_shift_right, Op.bitwise_and), incs=[(VES, 1)])
                S.emit("vector", lambda e: e.tensor_tensor(uq[:, :], uq[:, :], uz[:, :],
                                                           Op.subtract), incs=[(VES, 1)])
                qz_ve = S.c(VES)
                in0 = ap_with(uq, [part_dim(uq), [16, 8], [1, 16], [128, 8]])
                S.emit("vector", lambda e, a=in0: e.tensor_tensor(
                    dqf[:, :], a, srb[:, :], Op.mult),
                    waits=[(VES, qz_ve), ("c_srb", srl), (SES, S.c(SES))], incs=[(VES, 1)])
                dq_ve = S.c(VES)
                col = oc * 1024
                S.emit("scalar", lambda e, c_=col, d=dsth: e.copy(
                    out=d[:, c_:c_ + 1024], in_=dqf[:, :]),
                    waits=[(VES, dq_ve)], incs=[(SES, 1)])
                hse = S.c(SES)
                S.emit("vector", lambda e, c_=col, d=dsth, dl=dstl: e.tensor_tensor(
                    dl[:, c_:c_ + 1024], dqf[:, :], d[:, c_:c_ + 1024], Op.subtract),
                    waits=[(SES, hse)], incs=[(VES, 1)])
            return S.c(VES), S.c(SES)

        hid_evac_ve, hid_evac_se, hid_aq = {}, {}, {}

        def gu_mms(nt, sset, acc0, acc1, rdy_ve, rdy_se, prev_evac):
            dsth, dstl = slotset(sset)
            for kt in range(NKT):
                first, last = (kt == 0), (kt == NKT - 1)
                wgh = dsth[:, kt * 128:(kt + 1) * 128]
                wgl = dstl[:, kt * 128:(kt + 1) * 128]
                hh = lambda mc_, k_=kt: h_h[:, k_ * M + mc_ * 512: k_ * M + mc_ * 512 + 512]
                hl = lambda mc_, k_=kt: h_l[:, k_ * M + mc_ * 512: k_ * M + mc_ * 512 + 512]
                seq = [(wgh, hh(0), acc0, first, False), (wgh, hh(1), acc1, first, False),
                       (wgh, hl(0), acc0, False, False), (wgh, hl(1), acc1, False, False),
                       (wgl, hh(0), acc0, False, last), (wgl, hh(1), acc1, False, last)]
                for i_, (wv, hv, acc, st, sp) in enumerate(seq):
                    waits = []
                    if i_ == 0 and kt == 0:
                        waits = [(VES, rdy_ve), (SES, rdy_se)]
                        if prev_evac is not None:
                            waits += [(VES, prev_evac[0]), (SES, prev_evac[1])]
                    S.emit("tensor", lambda e, w_=wv, h2=hv, a_=acc, st_=st, sp_=sp: e.matmul(
                        a_[:, :], w_, h2, start=st_, stop=sp_), waits=waits, incs=[(MMS, 1)])
            return S.c(MMS)

        gu_done = {("g", -1): S.c(MMS), ("u", -1): S.c(MMS)}
        for nt in range(NNT):
            par = nt % 2
            g0, g1 = PS[0 + 4 * par], PS[1 + 4 * par]
            u0, u1 = PS[2 + 4 * par], PS[3 + 4 * par]
            gve, gse = deq_gu(qg_i, zg_i, sg_i, nt, 0, gu_done[("g", nt - 1)])
            pe = (hid_evac_ve[nt - 2], hid_evac_se[nt - 2]) if nt >= 2 else None
            gu_done[("g", nt)] = gu_mms(nt, 0, g0, g1, gve, gse, pe)
            uve, use_ = deq_gu(qu_i, zu_i, su_i, nt, 1, gu_done[("u", nt - 1)])
            gu_done[("u", nt)] = gu_mms(nt, 1, u0, u1, uve, use_, pe)
            umm = gu_done[("u", nt)]
            for mc, (gp, up_) in enumerate(((g0, u0), (g1, u1))):
                waits = [(MMS, umm), (VES, S.c(VES)), ("c_hs", S.c("c_hs"))]
                S.emit("scalar", lambda e, g_=gp: e.activation(
                    out=silt[:, :], in_=g_[:, :], func=AF.Silu),
                    waits=waits, incs=[(SES, 1)])
                sse = S.c(SES)
                S.emit("vector", lambda e, u2=up_: e.tensor_tensor(
                    hidf[:, :], silt[:, :], u2[:, :], Op.mult),
                    waits=[(SES, sse)], incs=[(VES, 1)])
                hfve = S.c(VES)
                S.emit("scalar", lambda e: e.copy(out=hidb_h[:, :], in_=hidf[:, :]),
                       waits=[(VES, hfve)], incs=[(SES, 1)])
                hbse = S.c(SES)
                S.emit("vector", lambda e: e.tensor_tensor(
                    hidb_l[:, :], hidf[:, :], hidb_h[:, :], Op.subtract),
                    waits=[(SES, hbse)], incs=[(VES, 1)])
                S.dma("gpsimd", hidh[nt, :, mc * 512:(mc + 1) * 512], hidb_h[:, :], "c_hs",
                      waits=[(SES, hbse)])
                S.dma("gpsimd", hidl[nt, :, mc * 512:(mc + 1) * 512], hidb_l[:, :], "c_hs",
                      waits=[(VES, S.c(VES))])
            hid_evac_ve[nt] = S.c(VES)
            hid_evac_se[nt] = S.c(SES)
            hid_aq[nt] = S.c("c_hs")
        hid_all_aq = S.c("c_hs")
        gu_all_mm = S.c(MMS)

        srb5 = scr[:, 0:1024]
        dqf5 = scr[:, 2048:2560]
        outb = [scr[:, 2560:3072], scr[:, 3072:3584]]
        x2ch4 = scr[:, 3584:4096]

        def deq_d(nt_list, dc, sset, prev_mm_done):
            dsth, dstl = slotset(sset)
            n8 = len(nt_list)
            for pr in range((n8 + 1) // 2):
                nts = nt_list[pr * 2:pr * 2 + 2]
                nn = len(nts)
                w_ = 64 * nn
                waits = [(MMS, prev_mm_done), (VES, S.c(VES))]
                S.dma("sync", qb[:, 0:w_],
                      ap_with(qd_i, [[64, 128], [8192, nn], [1, 64]],
                              dc * NNT * 8192 + nts[0] * 8192), "c_qz", waits=waits)
                S.dma("sync", zb[:, 0:w_],
                      ap_with(zd_i, [[64, 128], [8192, nn], [1, 64]],
                              dc * NNT * 8192 + nts[0] * 8192), "c_qz")
                qzl = S.c("c_qz")
                S.dma("scalar", srb5[:, 0:512 * nn],
                      ap_with(sd_i, [[512, 128], [65536, nn], [1, 512]],
                              dc * NNT * 65536 + nts[0] * 65536), "c_srb",
                      waits=[(VES, S.c(VES))])
                srl = S.c("c_srb")
                for i in range(8):
                    S.emit("vector", lambda e, i_=i, ww=w_: e.tensor_scalar(
                        uq[:, i_ * 128:i_ * 128 + ww], qb[:, 0:ww], 4 * i_, 15,
                        Op.logical_shift_right, Op.bitwise_and),
                        waits=[("c_qz", qzl)] if i == 0 else [], incs=[(VES, 1)])
                for i in range(8):
                    S.emit("vector", lambda e, i_=i, ww=w_: e.tensor_scalar(
                        uz[:, i_ * 128:i_ * 128 + ww], zb[:, 0:ww], 4 * i_, 15,
                        Op.logical_shift_right, Op.bitwise_and), incs=[(VES, 1)])
                S.emit("vector", lambda e: e.tensor_tensor(uq[:, :], uq[:, :], uz[:, :],
                                                           Op.subtract), incs=[(VES, 1)])
                qz_ve = S.c(VES)
                for ns in range(nn):
                    nt_loc = pr * 2 + ns
                    in0 = ap_with(uq, [part_dim(uq), [1, 64], [128, 8]], ns * 64)
                    S.emit("vector", lambda e, a=in0, ns_=ns: e.tensor_tensor(
                        dqf5[:, :], a, srb5[:, ns_ * 512:(ns_ + 1) * 512], Op.mult),
                        waits=[(VES, qz_ve), ("c_srb", srl), (SES, S.c(SES))], incs=[(VES, 1)])
                    S.emit("vector", lambda e, d_=dc: e.tensor_tensor(
                        dqf5[:, :], dqf5[:, :], gb_b[:, d_ * 512:(d_ + 1) * 512], Op.mult),
                        incs=[(VES, 1)])
                    dq_ve = S.c(VES)
                    col = nt_loc * 512
                    S.emit("scalar", lambda e, c_=col, d=dsth: e.copy(
                        out=d[:, c_:c_ + 512], in_=dqf5[:, :]),
                        waits=[(VES, dq_ve)], incs=[(SES, 1)])
                    hse = S.c(SES)
                    S.emit("vector", lambda e, c_=col, d=dsth, dl=dstl: e.tensor_tensor(
                        dl[:, c_:c_ + 512], dqf5[:, :], d[:, c_:c_ + 512], Op.subtract),
                        waits=[(SES, hse)], incs=[(VES, 1)])
            return S.c(VES), S.c(SES)

        dc_evac_se = {}
        oct_mm = {}
        x2tr_done = {}
        for dc in range(NDC):
            for mt in range(NMT):
                waits = []
                if dc == 0 and mt == 0:
                    waits = [("c_hs", hid_all_aq), (MMS, gu_all_mm)]
                if mt > 0:
                    waits.append((MMS, x2tr_done[(dc, mt - 1)]))
                elif dc > 0:
                    waits.append((MMS, x2tr_done[(dc - 1, NMT - 1)]))
                if dc > 0:
                    waits.append((SES, dc_evac_se[(dc - 1, mt)]))
                S.dma("gpsimd", x2ch4[:, :],
                      ap_with(x2s, [[M, 128], [128 * M, 4], [1, 128]],
                              (4 * dc) * 128 * M + mt * 128),
                      "c_x2c", waits=waits)
                xld = S.c("c_x2c")
                for j in range(4):
                    waits = [("c_x2c", xld)] if j == 0 else []
                    if j == 0 and dc > 0:
                        waits.append((SES, dc_evac_se[(dc - 1, mt)]))
                    S.emit("tensor", lambda e, m_=mt, j_=j: e.matmul(
                        PS[m_][:, j_ * 128:(j_ + 1) * 128],
                        x2ch4[:, j_ * 128:(j_ + 1) * 128], ident[:, :],
                        is_transpose=True, start=(j_ == 0), stop=False),
                        waits=waits, incs=[(MMS, 1)])
                x2tr_done[(dc, mt)] = S.c(MMS)
            x2tr_mm = S.c(MMS)
            for oi, oct_ in enumerate(OCTS):
                gslot = oi % 2
                n8 = len(oct_)
                waits = [(MMS, x2tr_mm)] if oi < 2 else [(MMS, oct_mm[oi - 2])]
                hchan = "c_ho%d" % (oi % 2)
                hh_base = h_h[:, 0:1]
                hl_base = h_l[:, 0:1]
                S.dma("gpsimd",
                      ap_with(hh_base, [part_dim(hh_base), [1024, n8], [1, 1024]],
                              gslot * 8192),
                      ap_with(hidh, [[M, 128], [128 * M, n8], [1, M]], oct_[0] * 128 * M),
                      hchan, waits=waits)
                S.dma("gpsimd",
                      ap_with(hl_base, [part_dim(hl_base), [1024, n8], [1, 1024]],
                              gslot * 8192),
                      ap_with(hidl, [[M, 128], [128 * M, n8], [1, M]], oct_[0] * 128 * M),
                      hchan)
                hid_ld = S.c(hchan)
                dve_, dse_ = deq_d(oct_, dc, gslot,
                                   x2tr_mm if oi < 2 else oct_mm[oi - 2])
                dsth, dstl = slotset(gslot)
                for ni, nt in enumerate(oct_):
                    for mt in range(NMT):
                        off = gslot * 8192 + ni * 1024 + mt * 128
                        lhs_h = h_h[:, off:off + 128]
                        lhs_l = h_l[:, off:off + 128]
                        wdh = dsth[:, ni * 512:(ni + 1) * 512]
                        wdl = dstl[:, ni * 512:(ni + 1) * 512]
                        last = (nt == NNT - 1)
                        seq = [(lhs_h, wdh, False), (lhs_h, wdl, False), (lhs_l, wdh, last)]
                        for i_, (lh, wv, sp) in enumerate(seq):
                            waits = []
                            if ni == 0 and mt == 0 and i_ == 0:
                                waits = [(VES, dve_), (SES, dse_), (hchan, hid_ld)]
                            S.emit("tensor", lambda e, l_=lh, w_=wv, m_=mt, sp_=sp: e.matmul(
                                PS[m_][:, :], l_, w_, start=False, stop=sp_),
                                waits=waits, incs=[(MMS, 1)])
                oct_mm[oi] = S.c(MMS)
            dn_mm = S.c(MMS)
            for mt in range(NMT):
                slot = mt % 2
                ochan = "c_ob%d" % slot
                waits = [(MMS, dn_mm)]
                if mt >= 2:
                    waits.append((ochan, S.c(ochan)))
                S.emit("scalar", lambda e, m_=mt, s_=slot: e.copy(
                    out=outb[s_][:, :], in_=PS[m_][:, :]),
                    waits=waits, incs=[(SES, 1)])
                dc_evac_se[(dc, mt)] = S.c(SES)
                S.dma("gpsimd", y_out[mt, :, dc * 512:(dc + 1) * 512], outb[slot][:, :],
                      ochan, waits=[(SES, S.c(SES))])

        S.emit("gpsimd", lambda e, v=S.c("c_ob0"): e.wait_ge(chan_sems["c_ob0"], v))
        S.emit("gpsimd", lambda e, v=S.c("c_ob1"): e.wait_ge(chan_sems["c_ob1"], v))

        sems = {"s_mm": s_mm, "s_ve": s_ve, "s_se": s_se, "s_cc": s_cc}
        sems.update(chan_sems)

        def replay(engine, key):
            for fn, waits, incs in S.ev[key]:
                for s, v in waits:
                    engine.wait_ge(sems[s], v)
                ins = fn(engine)
                if ins is not None:
                    for s, amt in incs:
                        ins = ins.then_inc(sems[s], amt)

        @block.sync
        def _(eng):
            replay(eng, "sync")

        @block.gpsimd
        def _(eng):
            replay(eng, "gpsimd")

        @block.scalar
        def _(eng):
            replay(eng, "scalar")

        @block.vector
        def _(eng):
            replay(eng, "vector")

        @block.tensor
        def _(eng):
            replay(eng, "tensor")

    return nc


_NC_CACHE = None
_LAST_RES = None


def kernel(**inputs):
    global _NC_CACHE
    x = np.ascontiguousarray(np.asarray(inputs["L0_attn_data"], dtype=np.float32).reshape(8192, D))
    W = np.asarray(inputs["L0_attn_weight"], dtype=np.float32)
    Mk = np.asarray(inputs["L0_attn_mask"], dtype=np.int32)
    w_t = np.ascontiguousarray(W.reshape(NKT, 128, NKT, 128).transpose(0, 2, 1, 3))
    m_t = np.ascontiguousarray(Mk.reshape(NKT, 128, NKT, 128).transpose(0, 2, 1, 3))

    def tile_gu(q):
        return np.ascontiguousarray(q.reshape(4, 8, 128, NNT, 16).transpose(3, 0, 2, 1, 4))

    def tile_gu_s(s):
        return np.ascontiguousarray(s.reshape(4, 8, 128, NNT, 128).transpose(3, 0, 2, 1, 4))

    qg = tile_gu(np.asarray(inputs["L0_ffn_gate_qweight"], dtype=np.int32))
    qu = tile_gu(np.asarray(inputs["L0_ffn_up_qweight"], dtype=np.int32))
    zg = tile_gu(np.repeat(np.asarray(inputs["L0_ffn_gate_qzeros"], dtype=np.int32), 128, axis=0))
    zu = tile_gu(np.repeat(np.asarray(inputs["L0_ffn_up_qzeros"], dtype=np.int32), 128, axis=0))
    sg = tile_gu_s(np.repeat(np.asarray(inputs["L0_ffn_gate_scales"], dtype=np.float32), 128, axis=0))
    su = tile_gu_s(np.repeat(np.asarray(inputs["L0_ffn_up_scales"], dtype=np.float32), 128, axis=0))

    def tile_d(q, cw):
        return np.ascontiguousarray(q.reshape(NNT, 128, NDC, cw).transpose(2, 0, 1, 3))

    qd = tile_d(np.asarray(inputs["L0_ffn_down_qweight"], dtype=np.int32), 64)
    zd = tile_d(np.repeat(np.asarray(inputs["L0_ffn_down_qzeros"], dtype=np.int32), 128, axis=0), 64)
    sd = tile_d(np.repeat(np.asarray(inputs["L0_ffn_down_scales"], dtype=np.float32), 128, axis=0), 512)

    cv = np.ascontiguousarray(np.asarray(inputs["L0_c"], dtype=np.float32).reshape(1, D))
    wmod_full = np.asarray(inputs["L0_wmod"], dtype=np.float32)
    ones_a = np.ones((1, 128), dtype=np.float32)
    ident_a = np.eye(128, dtype=np.float32)

    if _NC_CACHE is None:
        _NC_CACHE = build_nc()
    nc = _NC_CACHE

    in_maps = []
    for c in range(NCORE):
        in_maps.append({
            "x0": np.ascontiguousarray(x[c * M:(c + 1) * M].reshape(NMT, 128, D)),
            "w_t": w_t, "m_t": m_t,
            "qg": qg, "qu": qu, "zg": zg, "zu": zu, "sg": sg, "su": su,
            "qd": qd, "zd": zd, "sd": sd,
            "cvec": cv,
            "wmod": np.ascontiguousarray(wmod_full[:, c * 3072:(c + 1) * 3072]),
            "ones_in": ones_a, "ident_in": ident_a,
        })
    res = run_bass_kernel_spmd(nc, in_maps, core_ids=list(range(NCORE)))
    global _LAST_RES
    _LAST_RES = res
    out = np.concatenate([r["y"].reshape(M, D) for r in res.results], axis=0)
    return out.reshape(1, 8192, D).astype(np.float32, copy=False)
